# version 13
# baseline (speedup 1.0000x reference)
import contextlib
import numpy as np
import ml_dtypes
import concourse.bass as bass
import concourse.mybir as mybir
from concourse.bass_utils import run_bass_kernel_spmd

F32 = mybir.dt.float32
BF16 = mybir.dt.bfloat16
AF = mybir.ActivationFunctionType
ALU = mybir.AluOpType
AX = mybir.AxisListType

N_CORES = 8
D = 1024
DC = 8
S = 256
H = 4096
HC = 32
T = 2048
B = 2
NCH = 8
TC = 256
NT = 512
GRP = 32
NSLOT = 12
EPS = 1e-5
ENGINES = ("tensor", "scalar", "vector", "sync", "gpsimd")


def build():
    nc = bass.Bass("TRN2", target_bir_lowering=False)

    x_d = nc.dram_tensor("x", [B, T, D], F32, kind="ExternalInput").ap()
    W_in = nc.dram_tensor("W_in", [D, S], F32, kind="ExternalInput").ap()
    b_in = nc.dram_tensor("b_in", [S], F32, kind="ExternalInput").ap()
    W_gate = nc.dram_tensor("W_gate", [D, D], F32, kind="ExternalInput").ap()
    b_gate = nc.dram_tensor("b_gate", [D], F32, kind="ExternalInput").ap()
    A_w = nc.dram_tensor("A", [S, S], F32, kind="ExternalInput").ap()
    Bm_w = nc.dram_tensor("Bm", [S, S], F32, kind="ExternalInput").ap()
    C_w = nc.dram_tensor("Cw", [D, S], F32, kind="ExternalInput").ap()
    D_w = nc.dram_tensor("Dw", [D, S], F32, kind="ExternalInput").ap()
    W_out = nc.dram_tensor("W_out", [D, D], F32, kind="ExternalInput").ap()
    b_out = nc.dram_tensor("b_out", [D], F32, kind="ExternalInput").ap()
    w1w = nc.dram_tensor("w1w", [H, D], F32, kind="ExternalInput").ap()
    w1s = nc.dram_tensor("w1s", [H], F32, kind="ExternalInput").ap()
    w2w = nc.dram_tensor("w2w", [H, D], F32, kind="ExternalInput").ap()
    w2s = nc.dram_tensor("w2s", [H], F32, kind="ExternalInput").ap()
    h1w = nc.dram_tensor("h1w", [H, H], F32, kind="ExternalInput").ap()
    h1s = nc.dram_tensor("h1s", [H], F32, kind="ExternalInput").ap()
    h2w = nc.dram_tensor("h2w", [H, H], F32, kind="ExternalInput").ap()
    h2s = nc.dram_tensor("h2s", [H], F32, kind="ExternalInput").ap()
    w3w = nc.dram_tensor("w3w", [D, H], F32, kind="ExternalInput").ap()
    w3s = nc.dram_tensor("w3s", [D], F32, kind="ExternalInput").ap()
    identf_d = nc.dram_tensor("identf", [128, 128], F32, kind="ExternalInput").ap()
    y_d = nc.dram_tensor("y", [B, T, D], F32, kind="ExternalOutput").ap()
    scWg = nc.dram_tensor("scWg", [D, D], BF16, kind="Internal").ap()
    scWo = nc.dram_tensor("scWo", [D, D], BF16, kind="Internal").ap()
    scW1 = nc.dram_tensor("scW1", [H, D], BF16, kind="Internal").ap()
    scW2 = nc.dram_tensor("scW2", [H, D], BF16, kind="Internal").ap()
    scH1 = nc.dram_tensor("scH1", [H, H], BF16, kind="Internal").ap()
    scH2 = nc.dram_tensor("scH2", [H, H], BF16, kind="Internal").ap()
    scW3 = nc.dram_tensor("scW3", [D, H], BF16, kind="Internal").ap()
    scC = nc.dram_tensor("scC", [D, S], BF16, kind="Internal").ap()
    scD = nc.dram_tensor("scD", [D, S], BF16, kind="Internal").ap()
    sc2w1 = nc.dram_tensor("sc2w1", [8, 8, 128, 512], BF16, kind="Internal").ap()
    sc2w2 = nc.dram_tensor("sc2w2", [8, 8, 128, 512], BF16, kind="Internal").ap()
    sc2h1 = nc.dram_tensor("sc2h1", [8, 32, 128, 512], BF16, kind="Internal").ap()
    sc2h2 = nc.dram_tensor("sc2h2", [8, 32, 128, 512], BF16, kind="Internal").ap()
    sc2w3 = nc.dram_tensor("sc2w3", [2, 32, 128, 512], BF16, kind="Internal").ap()

    es = contextlib.ExitStack()
    sb = lambda name, shape, dt: es.enter_context(nc.sbuf_tensor(name, shape, dt))
    ps = lambda name, shape, dt: es.enter_context(nc.psum_tensor(name, shape, dt))

    wIn_sb = sb("wIn_sb", [128, 2048], BF16)
    A_sb = sb("A_sb", [128, 512], BF16)
    Bm_sb = sb("Bm_sb", [128, 512], BF16)
    CT_sb = sb("CT_sb", [128, 2048], BF16)
    DT_sb = sb("DT_sb", [128, 2048], BF16)
    idf_sb = sb("idf_sb", [128, 128], F32)
    bin_sb = sb("bin_sb", [128, 2], F32)
    bgate_sb = sb("bgate_sb", [128, 8], F32)
    bout_sb = sb("bout_sb", [128, 8], F32)
    w1s_sb = sb("w1s_sb", [128, 32], F32)
    w2s_sb = sb("w2s_sb", [128, 32], F32)
    h1s_sb = sb("h1s_sb", [128, 32], F32)
    h2s_sb = sb("h2s_sb", [128, 32], F32)
    h1se_sb = sb("h1se_sb", [128, 32], F32)
    h2se_sb = sb("h2se_sb", [128, 32], F32)
    w3s_sb = sb("w3s_sb", [128, 8], F32)
    x_tm = sb("x_tm", [128, 4096], F32)
    x2_tm = sb("x2_tm", [128, 4096], F32)
    tm1 = sb("tm1", [128, 4096], F32)
    tm2 = sb("tm2", [128, 4096], F32)
    x1_fm = sb("x1_fm", [128, 4096], BF16)
    ysx3_fm = sb("ysx3_fm", [128, 4096], BF16)
    g_fm = sb("g_fm", [128, 4096], BF16)
    uT = sb("uT", [128, 1024], BF16)
    hT = [sb("hT0", [128, 1024], BF16), sb("hT1", [128, 1024], BF16)]
    h_buf = sb("h_buf", [128, 16384], BF16)
    g1_buf = sb("g1_buf", [128, 16384], BF16)
    ring = sb("ring", [128, 512 * NSLOT], BF16)
    tmp = sb("tmp", [128, 2048], F32)
    stage = sb("stage", [128, 1024], F32)
    mk = sb("mk", [128, 4], F32)
    stg = sb("stg", [128, 4096], BF16)
    stat = sb("stat", [128, 56], F32)

    pf = [ps(f"pf{i}", [128, 512], F32) for i in range(4)]
    pscan = ps("pscan", [128, 512], F32)
    pab = [ps("pa", [128, 512], F32), ps("pb", [128, 512], F32)]
    ptp = ps("ptp", [128, 512], F32)

    sem_names = (["s_x", "s_st", "s_ct", "s_slabf"]
                 + [f"s_slot{i}" for i in range(NSLOT)]
                 + ["s_prepmisc", "s_pWg", "s_pC", "s_pD", "s_pW1", "s_pW2",
                    "s_pWo", "s_pH1", "s_pH2", "s_pW3",
                    "s_sgt0", "s_sgt1", "s_sgt2", "s_sgt3",
                    "s_sgw0", "s_sgw1", "s_sgw2", "s_sgw3",
                    "s_pH1p0", "s_pH1p1", "s_pH1p2", "s_pH1p3",
                    "s_pH1p4", "s_pH1p5", "s_pH1p6", "s_pH1p7",
                    "s_pH2p0", "s_pH2p1", "s_pH2p2", "s_pH2p3",
                    "s_pH2p4", "s_pH2p5", "s_pH2p6", "s_pH2p7",
                    "scanp", "scanh", "pa_r", "pa_f", "pb_r", "pb_f",
                    "pf0_r", "pf0_f", "pf1_r", "pf1_f", "pf2_r", "pf2_f",
                    "pf3_r", "pf3_f", "ptp_r", "ptp_f", "dvx", "acx", "vs"])
    SEM = {n: es.enter_context(nc.semaphore(n)) for n in sem_names}
    CNT = {n: 0 for n in sem_names}

    plan = {e: [] for e in ENGINES}
    pend = {e: None for e in ENGINES}

    def W(eng, sem, val):
        if val <= 0:
            return
        assert val <= 10**9
        plan[eng].append(("w", sem, val))

    def OP(eng, builder, inc=None):
        p = pend[eng]
        if inc is None and p is not None:
            inc = p
            pend[eng] = None
        plan[eng].append(("o", builder, inc))

    def bump(sem, amt=1):
        CNT[sem] += amt
        return CNT[sem]

    def defer_inc(eng, sem, amt):
        assert pend[eng] is None
        pend[eng] = (sem, bump(sem, amt) and (sem, amt))[1]

    def emit_xload(c):
        for tt in range(4):
            b, off = tt // 2, (tt % 2) * 128
            OP("scalar", (lambda E, b=b, off=off, c=c:
                E.dma_start(out=x_tm[:, (b * 2 + off // 128) * 1024:(b * 2 + off // 128 + 1) * 1024],
                            in_=x_d[b, c * TC + off: c * TC + off + 128, :])),
               ("s_x", bump("s_x", 16) and ("s_x", 16))[1])

    emit_xload(0)

    prep_marks = {}

    def prep_dma(builder, mark=None, sem="s_prepmisc"):
        OP("gpsimd", builder, (sem, bump(sem, 16) and (sem, 16))[1])
        if mark:
            prep_marks[mark] = (sem, CNT[sem])

    prep_dma(lambda E: E.dma_start(out=idf_sb[:], in_=identf_d[:]))
    prep_dma(lambda E: E.dma_start(
        out=wIn_sb[:].rearrange("p (kc m) -> p kc m", kc=DC),
        in_=W_in.rearrange("(kc p) o -> p kc o", p=128)), mark="wIn")
    prep_dma(lambda E: E.dma_start(
        out=A_sb[:].rearrange("p (si m) -> p si m", si=2),
        in_=A_w.rearrange("(si p) o -> p si o", p=128)), mark="A")
    prep_dma(lambda E: E.dma_start(
        out=Bm_sb[:].rearrange("p (si m) -> p si m", si=2),
        in_=Bm_w.rearrange("(si p) o -> p si o", p=128)), mark="Bm")
    stage_specs = [("b_in", b_in, 2, S), ("b_gate", b_gate, 8, D), ("b_out", b_out, 8, D),
                   ("w1s", w1s, 32, H), ("w2s", w2s, 32, H),
                   ("h1s", h1s, 32, H), ("h2s", h2s, 32, H), ("w3s", w3s, 8, D)]
    for j, (nm, src, k, n) in enumerate(stage_specs):
        prep_dma(lambda E, src=src, k=k, j=j: E.dma_start(
            out=tmp[0:k, j * 128:(j + 1) * 128],
            in_=src.rearrange("(k p) -> k p", p=128)), mark=nm)
    KIND_SEM = {"scWg": "s_pWg", "scC": "s_pC", "scD": "s_pD", "scW1": "s_pW1",
                "scW2": "s_pW2", "scWo": "s_pWo", "scH1": "s_pH1", "scH2": "s_pH2",
                "scW3": "s_pW3"}

    def cast_parts(dst, src, rows, cols, parts, mark):
        step = rows // parts
        for i in range(parts):
            prep_dma(lambda E, i=i, dst=dst, src=src, step=step:
                     E.dma_start(out=dst[i * step:(i + 1) * step, :],
                                 in_=src[i * step:(i + 1) * step, :]),
                     mark=mark if i == parts - 1 else None, sem=KIND_SEM[mark])

    cast_parts(scWg, W_gate, D, D, 2, "scWg")
    cast_parts(scC, C_w, D, S, 1, "scC")
    cast_parts(scD, D_w, D, S, 1, "scD")
    cast_parts(scW2, w2w, H, D, 4, "scW2")
    cast_parts(scW1, w1w, H, D, 4, "scW1")
    cast_parts(scWo, W_out, D, D, 2, "scWo")
    cast_parts(scW3, w3w, D, H, 4, "scW3")

    W("sync", *prep_marks["scC"])
    for sc in range(2):
        OP("sync", lambda E, sc=sc: E.dma_start(
            out=CT_sb[:, sc * 1024:(sc + 1) * 1024],
            in_=scC[:, sc * 128:(sc + 1) * 128], transpose=True),
           ("s_ct", bump("s_ct", 16) and ("s_ct", 16))[1])
    W("sync", *prep_marks["scD"])
    for sc in range(2):
        OP("sync", lambda E, sc=sc: E.dma_start(
            out=DT_sb[:, sc * 1024:(sc + 1) * 1024],
            in_=scD[:, sc * 128:(sc + 1) * 128], transpose=True),
           ("s_ct", bump("s_ct", 16) and ("s_ct", 16))[1])

    stage_plan = []
    for kind, srcT, dst, ngrp, nkc in [("scW2", scW2, sc2w2, 8, 8), ("scW1", scW1, sc2w1, 8, 8),
                                       ("scW3", scW3, sc2w3, 2, 32),
                                       ("scH1", scH1, sc2h1, 8, 32), ("scH2", scH2, sc2h2, 8, 32)]:
        for ms in range(0, ngrp, 2):
            for kc in range(nkc):
                stage_plan.append((kind, srcT, dst, ms, kc))
    sg_gate = {}

    OP("vector", lambda E: E.memset(hT[0][:], 0.0))
    OP("vector", lambda E: E.memset(hT[1][:], 0.0),
       ("dvx", bump("dvx") and ("dvx", 1))[1])
    K_memset = CNT["dvx"]

    W("tensor", *prep_marks["w3s"])
    offs = []
    off = 0
    for nm, src, k, n in stage_specs:
        offs.append((nm, off, k))
        off += k
    for j, (nm, o, k) in enumerate(offs):
        OP("tensor", (lambda E, j=j, o=o, k=k:
            E.transpose(ptp[:, o:o + k], tmp[0:k, j * 128:(j + 1) * 128], idf_sb[0:k, 0:k])),
           None if j < len(offs) - 1 else ("ptp_r", bump("ptp_r") and ("ptp_r", 1))[1])
    bias_dst = {"b_in": bin_sb, "b_gate": bgate_sb, "b_out": bout_sb,
                "w1s": w1s_sb, "w2s": w2s_sb, "h1s": h1s_sb, "h2s": h2s_sb, "w3s": w3s_sb}
    W("vector", "ptp_r", CNT["ptp_r"])
    for j, (nm, o, k) in enumerate(offs):
        OP("vector", (lambda E, nm=nm, o=o, k=k:
            E.tensor_copy(bias_dst[nm][:], ptp[:, o:o + k])),
           None if j < len(offs) - 1 else ("ptp_f", bump("ptp_f") and ("ptp_f", 1))[1])
    OP("vector", lambda E: E.tensor_scalar_mul(h1se_sb[:], h1s_sb[:], float(1.0 / np.sqrt(2.0))))
    OP("vector", lambda E: E.tensor_scalar_mul(h2se_sb[:], h2s_sb[:], float(1.0 / np.sqrt(2.0))),
       ("vs", bump("vs") and ("vs", 1))[1])
    K_scales = CNT["vs"]
    tp_round_bias_done = True

    slab_idx = [0]
    slab_first_wait = set()

    def slab_load(kind, src, row0, col0, transpose):
        assert not transpose
        j = slab_idx[0]
        slab_idx[0] += 1
        slot = j % NSLOT
        if kind not in slab_first_wait:
            slab_first_wait.add(kind)
            W("sync", *prep_marks[kind])
        if j >= NSLOT:
            W("sync", "s_slabf", j - NSLOT + 1)
        semn = f"s_slot{slot}"
        OP("sync", (lambda E, slot=slot, src=src, row0=row0, col0=col0:
            E.dma_start(out=ring[:, slot * 512:(slot + 1) * 512],
                        in_=src[row0:row0 + 128, col0:col0 + 512])),
           (semn, bump(semn, 16) and (semn, 16))[1])
        return slot, (semn, CNT[semn])

    def slab_load_quad(kind, sc2, ms, kc, nq):
        j = slab_idx[0]
        assert j % nq == 0 and NSLOT % nq == 0
        slab_idx[0] += nq
        slot = j % NSLOT
        key = (kind, (ms // 2) * 2)
        if key not in slab_first_wait:
            slab_first_wait.add(key)
            for q_ in range(4):
                W("sync", f"s_sgw{q_}", sg_gate[key][q_])
        if j + nq - 1 >= NSLOT:
            W("sync", "s_slabf", j + nq - NSLOT)
        semn = f"s_slot{slot}"
        OP("sync", (lambda E, slot=slot, sc2=sc2, ms=ms, kc=kc, nq=nq:
            E.dma_start(out=ring[:, slot * 512:(slot + nq) * 512],
                        in_=sc2[ms, kc:kc + nq].rearrange("k p m -> p k m"))),
           (semn, bump(semn, 16) and (semn, 16))[1])
        av = (semn, CNT[semn])
        return [(slot + q_, av) for q_ in range(nq)]

    def slab(slot):
        return ring[:, slot * 512:(slot + 1) * 512]

    bank_uses = [0, 0]
    rr = [0]

    def next_bank():
        i = rr[0] % 2
        rr[0] += 1
        return i

    def te_begin_round(i):
        W("tensor", ("pa_f", "pb_f")[i], bank_uses[i])

    def te_end_round(i):
        return (("pa_r", "pb_r")[i], bump(("pa_r", "pb_r")[i]) and (("pa_r", "pb_r")[i], 1))[1]

    def consumer_wait(eng, i):
        W(eng, ("pa_r", "pb_r")[i], bank_uses[i] + 1)

    def consumer_done(i):
        bank_uses[i] += 1
        return (("pa_f", "pb_f")[i], bump(("pa_f", "pb_f")[i]) and (("pa_f", "pb_f")[i], 1))[1]

    tp_round = [1]

    def emit_transpose_round(srcs, dsts, dve_inc_extra=None, te_wait=None):
        k = tp_round[0]
        tp_round[0] += 1
        if te_wait is not None:
            W("tensor", te_wait[0], te_wait[1])
        W("tensor", "ptp_f", k)
        n = len(srcs)
        for q, s_ in enumerate(srcs):
            OP("tensor", (lambda E, q=q, s_=s_:
                E.transpose(ptp[:, q * 128:(q + 1) * 128], s_, idf_sb[:])),
               None if q < n - 1 else ("ptp_r", bump("ptp_r") and ("ptp_r", 1))[1])
        W("vector", "ptp_r", CNT["ptp_r"])
        for q, d_ in enumerate(dsts):
            last = q == n - 1
            inc = None
            if last:
                inc = ("ptp_f", bump("ptp_f") and ("ptp_f", 1))[1]
            OP("vector", (lambda E, q=q, d_=d_:
                E.tensor_copy(d_, ptp[:, q * 128:(q + 1) * 128])), inc)
            if last and dve_inc_extra is not None:
                pass

    def vs_edge():
        OP("vector", lambda E: E.tensor_copy(mk[:, 0:1], mk[:, 0:1]),
           ("vs", bump("vs") and ("vs", 1))[1])
        W("vector", "vs", CNT["vs"])

    ln_count = [0]

    def ln_chain(g0):
        OP("vector", lambda E, g0=g0: E.tensor_add(stat[:, g0 + 4:g0 + 8], stat[:, g0 + 4:g0 + 8], stat[:, g0 + 8:g0 + 12]))
        OP("vector", lambda E, g0=g0: E.tensor_scalar_mul(stat[:, g0 + 12:g0 + 16], stat[:, g0 + 0:g0 + 4], 1.0 / D))
        vs_edge()
        OP("vector", lambda E, g0=g0: E.tensor_mul(stat[:, g0 + 24:g0 + 28], stat[:, g0 + 12:g0 + 16], stat[:, g0 + 12:g0 + 16]))
        vs_edge()
        OP("vector", lambda E, g0=g0: E.scalar_tensor_tensor(
            stat[:, g0 + 16:g0 + 20], stat[:, g0 + 4:g0 + 8], 1.0 / D, stat[:, g0 + 24:g0 + 28], ALU.mult, ALU.subtract))
        vs_edge()
        OP("vector", lambda E, g0=g0: E.tensor_scalar_add(stat[:, g0 + 16:g0 + 20], stat[:, g0 + 16:g0 + 20], EPS))
        vs_edge()
        I32 = mybir.dt.int32
        OP("vector", lambda E, g0=g0: E.tensor_single_scalar(
            stat[:, g0 + 24:g0 + 28].bitcast(I32), stat[:, g0 + 16:g0 + 20].bitcast(I32),
            1, ALU.logical_shift_right))
        vs_edge()
        OP("vector", lambda E, g0=g0: E.tensor_scalar(
            stat[:, g0 + 20:g0 + 24].bitcast(I32), stat[:, g0 + 24:g0 + 28].bitcast(I32),
            -1, 0x5F3759DF, ALU.mult, ALU.add))
        yield
        for _ in range(3):
            vs_edge()
            OP("vector", lambda E, g0=g0: E.tensor_mul(stat[:, g0 + 24:g0 + 28], stat[:, g0 + 20:g0 + 24], stat[:, g0 + 20:g0 + 24]))
            vs_edge()
            OP("vector", lambda E, g0=g0: E.scalar_tensor_tensor(
                stat[:, g0 + 24:g0 + 28], stat[:, g0 + 24:g0 + 28], -0.5, stat[:, g0 + 16:g0 + 20], ALU.mult, ALU.mult))
            vs_edge()
            OP("vector", lambda E, g0=g0: E.scalar_tensor_tensor(
                stat[:, g0 + 20:g0 + 24], stat[:, g0 + 24:g0 + 28], 1.5, stat[:, g0 + 20:g0 + 24], ALU.add, ALU.mult))
            yield
        vs_edge()

    def emit_ln(src, dst_write, act_wait=None, dve_wait=None):
        g0 = (ln_count[0] % 2) * 28
        ln_count[0] += 1
        if act_wait is not None:
            W("scalar", act_wait[0], act_wait[1])
        for tt in range(4):
            for hf in range(2):
                last = tt == 3 and hf == 1
                col = g0 + (4 if hf == 0 else 8) + tt
                OP("scalar", (lambda E, tt=tt, hf=hf, col=col:
                    E.activation(out=stage[:, hf * 512:(hf + 1) * 512],
                                 in_=src[:, tt * 1024 + hf * 512: tt * 1024 + (hf + 1) * 512],
                                 func=AF.Square,
                                 accum_out=stat[:, col:col + 1])),
                   ("acx", bump("acx") and ("acx", 1))[1] if last else None)
        K_sq = CNT["acx"]
        if dve_wait is not None:
            W("vector", dve_wait[0], dve_wait[1])
        for tt in range(4):
            OP("vector", (lambda E, tt=tt, g0=g0:
                E.reduce_sum(stat[:, g0 + tt:g0 + tt + 1], src[:, tt * 1024:(tt + 1) * 1024], axis=AX.X)))
        W("vector", "acx", K_sq)
        for _ in ln_chain(g0):
            pass
        for tt in range(4):
            dst_write(tt, g0)

    def ln_apply(dst, src, tt, g0, inc=None):
        OP("vector", (lambda E, tt=tt, g0=g0:
            E.tensor_scalar(dst[:, tt * 1024:(tt + 1) * 1024], src[:, tt * 1024:(tt + 1) * 1024],
                            stat[:, g0 + 12 + tt:g0 + 13 + tt], stat[:, g0 + 20 + tt:g0 + 21 + tt],
                            ALU.subtract, ALU.mult)), inc)

    pf_uses = [0, 0, 0, 0]
    K_sq_ln4 = [0]
    u_marks = {}

    def emit_xload_gated(c, gate):
        if gate is not None:
            W("scalar", gate[0], gate[1])
        emit_xload(c)

    def emit_slabs(c):
        slabs_Wg = [[slab_load("scWg", scWg, kc * 128, h * 512, False)
                     for kc in range(8)] for h in range(2)]
        slabs_Wo = [[slab_load("scWo", scWo, kc * 128, h * 512, False)
                     for kc in range(8)] for h in range(2)]
        stg_gen = emit_staging() if c == 0 else None

        def pump_stage(n):
            if stg_gen is None:
                return
            for _ in range(n):
                try:
                    next(stg_gen)
                except StopIteration:
                    break

        def pair_group(kind, sc2, ms, nkc):
            out = []
            for kc in range(0, nkc, 4):
                out.extend(slab_load_quad(kind, sc2, ms, kc, 4))
            return out
        pump_stage(64)
        slabs_ffn = []
        for ms in range(8):
            w2sl = pair_group("scW2", sc2w2, ms, 8)
            w1sl = pair_group("scW1", sc2w1, ms, 8)
            slabs_ffn.append((w2sl, w1sl))
        pump_stage(10 ** 9)
        slabs_h1 = [pair_group("scH1", sc2h1, ms, 32) for ms in range(8)]
        slabs_h2 = [pair_group("scH2", sc2h2, ms, 32) for ms in range(8)]
        slabs_w3 = [pair_group("scW3", sc2w3, ms, 32) for ms in range(2)]
        return slabs_Wg, slabs_Wo, slabs_ffn, slabs_h1, slabs_h2, slabs_w3

    def gen_A(c):
        g0 = (ln_count[0] % 2) * 28
        ln_count[0] += 1
        xg = ("s_x", 64 * (c + 1))
        W("scalar", xg[0], xg[1])
        for tt in range(4):
            for hf in range(2):
                last = tt == 3 and hf == 1
                col = g0 + (4 if hf == 0 else 8) + tt
                OP("scalar", (lambda E, tt=tt, hf=hf, col=col:
                    E.activation(out=stage[:, hf * 512:(hf + 1) * 512],
                                 in_=x_tm[:, tt * 1024 + hf * 512: tt * 1024 + (hf + 1) * 512],
                                 func=AF.Square, accum_out=stat[:, col:col + 1])),
                   ("acx", bump("acx") and ("acx", 1))[1] if last else None)
        K_sq = CNT["acx"]
        yield
        W("vector", xg[0], xg[1])
        for tt in range(4):
            OP("vector", (lambda E, tt=tt, g0=g0:
                E.reduce_sum(stat[:, g0 + tt:g0 + tt + 1], x_tm[:, tt * 1024:(tt + 1) * 1024], axis=AX.X)))
        W("vector", "acx", K_sq)
        yield
        yield from ln_chain(g0)
        for tt in range(4):
            ln_apply(tm1, x_tm, tt, g0,
                     inc=("dvx", bump("dvx") and ("dvx", 1))[1] if tt == 3 else None)
        K_ln1 = CNT["dvx"]
        yield
        for dc in range(DC):
            emit_transpose_round(
                [tm1[:, tt * 1024 + dc * 128: tt * 1024 + dc * 128 + 128] for tt in range(4)],
                [x1_fm[:, dc * 512 + tt * 128: dc * 512 + (tt + 1) * 128] for tt in range(4)],
                te_wait=("dvx", K_ln1) if dc == 0 else None)
            yield
        K_x1fm = CNT["ptp_f"]
        marks = []
        W("tensor", "ptp_f", K_x1fm)
        if c == 0:
            W("tensor", *prep_marks["w3s"])
        for sc in range(2):
            i = next_bank()
            te_begin_round(i)
            for kc in range(DC):
                OP("tensor", (lambda E, i=i, kc=kc, sc=sc:
                    E.matmul(pab[i][:], wIn_sb[:, kc * 256 + sc * 128: kc * 256 + (sc + 1) * 128],
                             x1_fm[:, kc * 512:(kc + 1) * 512],
                             start=(kc == 0), stop=(kc == DC - 1))),
                   te_end_round(i) if kc == DC - 1 else None)
            consumer_wait("scalar", i)
            if c == 0 and sc == 0:
                W("scalar", "ptp_f", 1)
            OP("scalar", (lambda E, i=i, sc=sc:
                E.activation(out=uT[:, sc * 512:(sc + 1) * 512], in_=pab[i][:],
                             func=AF.Identity, bias=bin_sb[:, sc:sc + 1])),
               consumer_done(i))
            marks.append((("pa_f", "pb_f")[i], bank_uses[i]))
            yield
        u_marks[c] = marks

    def gen_scan(c):
        hcur, hprv = hT[c % 2], hT[(c + 1) % 2]
        if c == 0:
            W("tensor", "dvx", K_memset)
            W("tensor", *prep_marks["w3s"])
        W("tensor", u_marks[c][0][0], u_marks[c][0][1])
        W("tensor", u_marks[c][1][0], u_marks[c][1][1])
        base = c * TC
        for g_i in range(TC // GRP):
            gp = g_i % 2
            t0 = g_i * GRP
            W("tensor", "scanh", 2 * (base + t0))
            first = True
            for so in range(2):
                for b in range(2):
                    for si in range(2):
                        OP("tensor", (lambda E, gp=gp, so=so, b=b, si=si, t0=t0, first=first:
                            E.matmul(pscan[:, gp * 256 + so * 64 + b * 32: gp * 256 + so * 64 + (b + 1) * 32],
                                     Bm_sb[:, (si * 2 + so) * 128:(si * 2 + so + 1) * 128],
                                     uT[:, si * 512 + b * 256 + t0: si * 512 + b * 256 + t0 + GRP],
                                     start=first, stop=(so == 1 and b == 1 and si == 1),
                                     skip_group_check=True)))
                        first = False
            yield
            for dt_ in range(GRP):
                tt = t0 + dt_
                gt = base + tt
                if gt > 0:
                    W("tensor", "scanh", 2 * gt)
                pv = pscan[:, gp * 256: gp * 256 + 128].rearrange("p (so b t) -> p so b t", so=2, b=2)
                for so in range(2):
                    for si in range(2):
                        if tt == 0:
                            rhsb = (lambda E, si=si, hp=hprv:
                                    hp[:].rearrange("p (sc b t) -> p sc b t", sc=2, b=2)[:, si, :, TC - 1])
                        else:
                            rhsb = (lambda E, si=si, hc_=hcur, tt=tt:
                                    hc_[:].rearrange("p (sc b t) -> p sc b t", sc=2, b=2)[:, si, :, tt - 1])
                        OP("tensor", (lambda E, pv=pv, so=so, si=si, dt_=dt_, rhsb=rhsb:
                            E.matmul(pv[:, so, :, dt_],
                                     A_sb[:, (si * 2 + so) * 128:(si * 2 + so + 1) * 128],
                                     rhsb(E), start=False, stop=(si == 1),
                                     skip_group_check=True)),
                           ("scanp", bump("scanp") and ("scanp", 1))[1] if (so == 1 and si == 1) else None)
                W("scalar", "scanp", CNT["scanp"])
                OP("scalar", (lambda E, gp=gp, dt_=dt_, tt=tt, hc_=hcur:
                    E.activation(
                        out=hc_[:].rearrange("p (sc b t) -> p sc b t", sc=2, b=2)[:, :, :, tt],
                        in_=pscan[:, gp * 256: gp * 256 + 128]
                            .rearrange("p (so b t) -> p so b t", so=2, b=2)[:, :, :, dt_],
                        func=AF.Tanh)),
                   ("scanh", bump("scanh", 2) and ("scanh", 2))[1])
                yield

    def emit_C(c, slabs_Wg, slabs_Wo):
        hcur = hT[c % 2]
        if c == 0:
            W("tensor", "s_ct", 64)
        W("tensor", "scanh", 2 * TC * (c + 1))

        def g_round(dc):
            h = dc // 4
            i = next_bank()
            te_begin_round(i)
            for kc in range(DC):
                slot, avail = slabs_Wg[h][kc]
                if dc % 4 == 0:
                    W("tensor", avail[0], avail[1])
                inc = None
                if kc == DC - 1:
                    inc = te_end_round(i)
                elif dc % 4 == 3:
                    inc = ("s_slabf", bump("s_slabf") and ("s_slabf", 1))[1]
                OP("tensor", (lambda E, i=i, slot=slot, dc=dc, kc=kc:
                    E.matmul(pab[i][:], slab(slot)[:, (dc % 4) * 128:(dc % 4 + 1) * 128],
                             x1_fm[:, kc * 512:(kc + 1) * 512],
                             start=(kc == 0), stop=(kc == DC - 1))), inc)
            if dc % 4 == 3:
                defer_inc("tensor", "s_slabf", 1)
            consumer_wait("scalar", i)
            OP("scalar", (lambda E, i=i, dc=dc:
                E.activation(out=g_fm[:, dc * 512:(dc + 1) * 512], in_=pab[i][:],
                             func=AF.Sigmoid, bias=bgate_sb[:, dc:dc + 1])),
               consumer_done(i))

        def y_pair(dp):
            banks = []
            for do in (2 * dp, 2 * dp + 1):
                i = next_bank()
                banks.append((do, i))
                te_begin_round(i)
                for j in range(4):
                    mat, src_, sc = ((CT_sb, hcur, 0), (CT_sb, hcur, 1),
                                     (DT_sb, uT, 0), (DT_sb, uT, 1))[j]
                    OP("tensor", (lambda E, i=i, mat=mat, src_=src_, sc=sc, do=do:
                        E.matmul(pab[i][:], mat[:, sc * 1024 + do * 128: sc * 1024 + (do + 1) * 128],
                                 src_[:, sc * 512:(sc + 1) * 512],
                                 start=(mat is CT_sb and sc == 0), stop=(mat is DT_sb and sc == 1))),
                       te_end_round(i) if j == 3 else None)
            for do, i in banks:
                consumer_wait("vector", i)
                sl = 0 if do % 2 == 0 else 1
                OP("vector", (lambda E, i=i, do=do, sl=sl:
                    E.scalar_tensor_tensor(tmp[:, sl * 512:(sl + 1) * 512], pab[i][:], 1.0,
                                           x1_fm[:, do * 512:(do + 1) * 512],
                                           ALU.mult, ALU.subtract)),
                   consumer_done(i))
            vs_edge()
            for do, i in banks:
                sl = 0 if do % 2 == 0 else 1
                OP("vector", (lambda E, do=do, sl=sl:
                    E.tensor_mul(tmp[:, (2 + sl) * 512:(3 + sl) * 512],
                                 tmp[:, sl * 512:(sl + 1) * 512],
                                 g_fm[:, do * 512:(do + 1) * 512])))
            vs_edge()
            for do, i in banks:
                sl = 0 if do % 2 == 0 else 1
                last = dp == 3 and do == 2 * dp + 1
                inc = ("dvx", bump("dvx") and ("dvx", 1))[1] if last else None
                OP("vector", (lambda E, do=do, sl=sl:
                    E.tensor_add(ysx3_fm[:, do * 512:(do + 1) * 512],
                                 tmp[:, (2 + sl) * 512:(3 + sl) * 512],
                                 x1_fm[:, do * 512:(do + 1) * 512])), inc)

        g_round(0); g_round(1); g_round(2); g_round(3)
        y_pair(0)
        g_round(4); g_round(5)
        y_pair(1)
        g_round(6); g_round(7)
        y_pair(2); y_pair(3)
        K_ys = CNT["dvx"]

        W("tensor", "dvx", K_ys)
        first_stt = [True]
        for do in range(DC):
            h = do // 4
            i = next_bank()
            te_begin_round(i)
            for kc in range(DC):
                slot, avail = slabs_Wo[h][kc]
                if do % 4 == 0:
                    W("tensor", avail[0], avail[1])
                inc = None
                if kc == DC - 1:
                    inc = te_end_round(i)
                elif do % 4 == 3:
                    inc = ("s_slabf", bump("s_slabf") and ("s_slabf", 1))[1]
                OP("tensor", (lambda E, i=i, slot=slot, do=do, kc=kc:
                    E.matmul(pab[i][:], slab(slot)[:, (do % 4) * 128:(do % 4 + 1) * 128],
                             ysx3_fm[:, kc * 512:(kc + 1) * 512],
                             start=(kc == 0), stop=(kc == DC - 1))), inc)
            if do % 4 == 3:
                defer_inc("tensor", "s_slabf", 1)
            consumer_wait("vector", i)
            if first_stt[0]:
                first_stt[0] = False
                if K_sq_ln4[0]:
                    W("vector", "acx", K_sq_ln4[0])
            if do >= 2:
                W("vector", "ptp_r", tp_round[0])
            OP("vector", (lambda E, i=i, do=do:
                E.scalar_tensor_tensor(stage[:, (do % 2) * 512:(do % 2 + 1) * 512],
                                       pab[i][:], bout_sb[:, do:do + 1],
                                       x1_fm[:, do * 512:(do + 1) * 512],
                                       ALU.add, ALU.add)),
               consumer_done(i))
            emit_transpose_round(
                [stage[:, (do % 2) * 512 + tt * 128: (do % 2) * 512 + (tt + 1) * 128] for tt in range(4)],
                [tm2[:, tt * 1024 + do * 128: tt * 1024 + (do + 1) * 128] for tt in range(4)],
                te_wait=(("pa_f", "pb_f")[i], bank_uses[i]))
        K_mixtm = CNT["ptp_f"]

        def ln2_write(tt, g0):
            if tt == 0 and c >= 1:
                W("vector", "s_st", 64 * c)
            ln_apply(x2_tm, tm2, tt, g0)
            if tt == 3:
                vs_edge()
                for t2_ in range(4):
                    OP("vector", (lambda E, t2_=t2_:
                        E.tensor_add(x2_tm[:, t2_ * 1024:(t2_ + 1) * 1024],
                                     x2_tm[:, t2_ * 1024:(t2_ + 1) * 1024],
                                     x_tm[:, t2_ * 1024:(t2_ + 1) * 1024])),
                       ("dvx", bump("dvx") and ("dvx", 1))[1] if t2_ == 3 else None)
        emit_ln(tm2, ln2_write, act_wait=("ptp_f", K_mixtm), dve_wait=("ptp_f", K_mixtm))
        K_x2 = CNT["dvx"]
        if c + 1 < NCH:
            emit_xload_gated(c + 1, ("dvx", K_x2))

        emit_ln(x2_tm, lambda tt, g0: ln_apply(tm1, x2_tm, tt, g0,
                inc=("dvx", bump("dvx") and ("dvx", 1))[1] if tt == 3 else None),
                act_wait=("dvx", K_x2), dve_wait=("dvx", K_x2))
        K_ln3 = CNT["dvx"]

        for dc in range(DC):
            emit_transpose_round(
                [tm1[:, tt * 1024 + dc * 128: tt * 1024 + dc * 128 + 128] for tt in range(4)],
                [ysx3_fm[:, dc * 512 + tt * 128: dc * 512 + (tt + 1) * 128] for tt in range(4)],
                te_wait=("dvx", K_ln3) if dc == 0 else None)
        return CNT["ptp_f"]

    def gen_D(c, K_x3fm, slabs_ffn, slabs_h1, slabs_h2, slabs_w3):
        def w12_round(go, bk, sl8, i_rel):
            W("tensor", f"pf{bk}_f", pf_uses[bk])
            for kc in range(8):
                slot, avail = sl8[kc]
                if go == 0:
                    W("tensor", avail[0], avail[1])
                inc = None
                if kc == 7:
                    inc = (f"pf{bk}_r", bump(f"pf{bk}_r") and (f"pf{bk}_r", 1))[1]
                elif i_rel:
                    inc = ("s_slabf", bump("s_slabf") and ("s_slabf", 1))[1]
                OP("tensor", (lambda E, bk=bk, slot=slot, go=go, kc=kc:
                    E.matmul(pf[bk][:], slab(slot)[:, go * 128:(go + 1) * 128],
                             ysx3_fm[:, kc * 512:(kc + 1) * 512],
                             start=(kc == 0), stop=(kc == 7))), inc)
            if i_rel:
                defer_inc("tensor", "s_slabf", 1)

        W("tensor", "ptp_f", K_x3fm)
        if c == 0:
            W("scalar", "vs", K_scales)
        for ms in range(8):
            w2sl, w1sl = slabs_ffn[ms]
            rot = 2 * (ms % 2)
            for go in range(4):
                hc = ms * 4 + go
                bk = (go + rot) % 4
                w12_round(go, bk, w2sl, go == 3)
                W("vector", f"pf{bk}_r", CNT[f"pf{bk}_r"])
                OP("vector", (lambda E, bk=bk, hc=hc:
                    E.tensor_scalar(h_buf[:, hc * 512:(hc + 1) * 512], pf[bk][:],
                                    w2s_sb[:, hc:hc + 1], None, ALU.mult)),
                   (f"pf{bk}_f", bump(f"pf{bk}_f") and (f"pf{bk}_f", 1))[1])
                pf_uses[bk] += 1
                yield
            for go in range(4):
                hc = ms * 4 + go
                bk = (go + rot) % 4
                w12_round(go, bk, w1sl, go == 3)
                W("scalar", f"pf{bk}_r", CNT[f"pf{bk}_r"])
                OP("scalar", (lambda E, bk=bk, hc=hc:
                    E.activation(out=tmp[:, bk * 512:(bk + 1) * 512], in_=pf[bk][:],
                                 func=AF.Sigmoid, scale=w1s_sb[:, hc:hc + 1])),
                   ("acx", bump("acx") and ("acx", 1))[1])
                W("vector", "acx", CNT["acx"])
                OP("vector", (lambda E, bk=bk, hc=hc:
                    E.scalar_tensor_tensor(tmp[:, bk * 512:(bk + 1) * 512], pf[bk][:],
                                           w1s_sb[:, hc:hc + 1], tmp[:, bk * 512:(bk + 1) * 512],
                                           ALU.mult, ALU.mult)))
                vs_edge()
                OP("vector", (lambda E, bk=bk, hc=hc:
                    E.tensor_mul(h_buf[:, hc * 512:(hc + 1) * 512],
                                 h_buf[:, hc * 512:(hc + 1) * 512],
                                 tmp[:, bk * 512:(bk + 1) * 512])),
                   (f"pf{bk}_f", bump(f"pf{bk}_f") and (f"pf{bk}_f", 1))[1])
                pf_uses[bk] += 1
                if hc == 31:
                    OP("vector", lambda E: E.tensor_copy(mk[:, 0:1], mk[:, 0:1]),
                       ("dvx", bump("dvx") and ("dvx", 1))[1])
                yield
        K_h = CNT["dvx"]

        def big_pass(slabs_m, rhs_buf, out_buf, se_sb, s_sb, te_extra_wait):
            for ms in range(8):
                rot = 2 * (ms % 2)
                if te_extra_wait is not None and ms == 0:
                    W("tensor", te_extra_wait[0], te_extra_wait[1])
                for kc in range(32):
                    slot, avail = slabs_m[ms][kc]
                    W("tensor", avail[0], avail[1])
                    for go in range(4):
                        bk = (go + rot) % 4
                        if kc == 0:
                            W("tensor", f"pf{bk}_f", pf_uses[bk])
                        inc = None
                        if kc == 31:
                            inc = (f"pf{bk}_r", bump(f"pf{bk}_r") and (f"pf{bk}_r", 1))[1]
                        elif go == 3:
                            inc = ("s_slabf", bump("s_slabf") and ("s_slabf", 1))[1]
                        OP("tensor", (lambda E, bk=bk, go=go, slot=slot, kc=kc, rhs_buf=rhs_buf:
                            E.matmul(pf[bk][:], slab(slot)[:, go * 128:(go + 1) * 128],
                                     rhs_buf[:, kc * 512:(kc + 1) * 512],
                                     start=(kc == 0), stop=(kc == 31))), inc)
                    yield
                defer_inc("tensor", "s_slabf", 1)
                for go in range(4):
                    hc = ms * 4 + go
                    bk = (go + rot) % 4
                    last = ms == 7 and go == 3
                    W("scalar", f"pf{bk}_r", CNT[f"pf{bk}_r"])
                    OP("scalar", (lambda E, bk=bk, hc=hc, se_sb=se_sb:
                        E.activation(out=tmp[:, bk * 512:(bk + 1) * 512], in_=pf[bk][:],
                                     func=AF.Erf, scale=se_sb[:, hc:hc + 1])),
                       ("acx", bump("acx") and ("acx", 1))[1])
                    W("vector", "acx", CNT["acx"])
                    OP("vector", (lambda E, bk=bk:
                        E.tensor_scalar(tmp[:, bk * 512:(bk + 1) * 512],
                                        tmp[:, bk * 512:(bk + 1) * 512], 1.0, 0.5,
                                        ALU.add, ALU.mult)))
                    vs_edge()
                    OP("vector", (lambda E, bk=bk, hc=hc, s_sb=s_sb, out_buf=out_buf:
                        E.scalar_tensor_tensor(out_buf[:, hc * 512:(hc + 1) * 512], pf[bk][:],
                                               s_sb[:, hc:hc + 1], tmp[:, bk * 512:(bk + 1) * 512],
                                               ALU.mult, ALU.mult)),
                       (f"pf{bk}_f", bump(f"pf{bk}_f") and (f"pf{bk}_f", 1))[1])
                    pf_uses[bk] += 1
                    if last:
                        OP("vector", lambda E: E.tensor_copy(mk[:, 0:1], mk[:, 0:1]),
                           ("dvx", bump("dvx") and ("dvx", 1))[1])
                yield

        yield from big_pass(slabs_h1, h_buf, g1_buf, h1se_sb, h1s_sb, ("dvx", K_h))
        K_g1 = CNT["dvx"]
        yield from big_pass(slabs_h2, g1_buf, h_buf, h2se_sb, h2s_sb, ("dvx", K_g1))
        K_g2 = CNT["dvx"]

        W("tensor", "dvx", K_g2)
        for ms in range(2):
            rot = 2 * (ms % 2)
            for kc in range(32):
                slot, avail = slabs_w3[ms][kc]
                W("tensor", avail[0], avail[1])
                for go in range(4):
                    bk = (go + rot) % 4
                    if kc == 0:
                        W("tensor", f"pf{bk}_f", pf_uses[bk])
                    inc = None
                    if kc == 31:
                        inc = (f"pf{bk}_r", bump(f"pf{bk}_r") and (f"pf{bk}_r", 1))[1]
                    elif go == 3:
                        inc = ("s_slabf", bump("s_slabf") and ("s_slabf", 1))[1]
                    OP("tensor", (lambda E, bk=bk, go=go, slot=slot, kc=kc:
                        E.matmul(pf[bk][:], slab(slot)[:, go * 128:(go + 1) * 128],
                                 h_buf[:, kc * 512:(kc + 1) * 512],
                                 start=(kc == 0), stop=(kc == 31))), inc)
                yield
            defer_inc("tensor", "s_slabf", 1)
            for go in range(4):
                do = ms * 4 + go
                bk = (go + rot) % 4
                W("scalar", f"pf{bk}_r", CNT[f"pf{bk}_r"])
                if do >= 2:
                    W("scalar", "ptp_r", tp_round[0])
                OP("scalar", (lambda E, bk=bk, do=do:
                    E.activation(out=stage[:, (do % 2) * 512:(do % 2 + 1) * 512], in_=pf[bk][:],
                                 func=AF.Identity, scale=w3s_sb[:, do:do + 1])),
                   (f"pf{bk}_f", bump(f"pf{bk}_f") and (f"pf{bk}_f", 1))[1])
                pf_uses[bk] += 1
                emit_transpose_round(
                    [stage[:, (do % 2) * 512 + tt * 128: (do % 2) * 512 + (tt + 1) * 128]
                     for tt in range(4)],
                    [tm2[:, tt * 1024 + do * 128: tt * 1024 + (do + 1) * 128] for tt in range(4)],
                    te_wait=(f"pf{bk}_f", CNT[f"pf{bk}_f"]))
                yield
        K_ffntm = CNT["ptp_f"]

        def ln4_write(tt, g0):
            ln_apply(tm2, tm2, tt, g0)
            if tt == 3:
                vs_edge()
                for t2_ in range(4):
                    OP("vector", (lambda E, t2_=t2_:
                        E.tensor_add(x2_tm[:, t2_ * 1024:(t2_ + 1) * 1024],
                                     x2_tm[:, t2_ * 1024:(t2_ + 1) * 1024],
                                     tm2[:, t2_ * 1024:(t2_ + 1) * 1024])),
                       ("dvx", bump("dvx") and ("dvx", 1))[1] if t2_ == 3 else None)
        emit_ln(tm2, ln4_write, act_wait=("ptp_f", K_ffntm), dve_wait=("ptp_f", K_ffntm))
        K_sq_ln4[0] = CNT["acx"] - 0
        K_xout = CNT["dvx"]

        W("gpsimd", "dvx", K_xout)
        for tt in range(4):
            b, off = tt // 2, (tt % 2) * 128
            OP("gpsimd", (lambda E, b=b, off=off, c=c:
                E.dma_start(out=y_d[b, c * TC + off: c * TC + off + 128, :],
                            in_=x2_tm[:, (b * 2 + off // 128) * 1024:(b * 2 + off // 128 + 1) * 1024])),
               ("s_st", bump("s_st", 16) and ("s_st", 16))[1])

    stage_state = {}

    def emit_staging():
        NSG = 4
        nw = [0] * NSG
        kind_waited2 = set()
        hcast_done = set()
        for t_i, (kind, srcT, dst, ms, kc) in enumerate(stage_plan):
            sl_ = t_i % NSG
            if kind in ("scH1", "scH2") and (kind, ms) not in hcast_done:
                hcast_done.add((kind, ms))
                hw_, scv, pn = (h1w, scH1, "s_pH1p") if kind == "scH1" else (h2w, scH2, "s_pH2p")
                for m2 in (ms, ms + 1):
                    OP("gpsimd", (lambda E, m2=m2, hw_=hw_, scv=scv: E.dma_start(
                        out=scv[m2 * 512:(m2 + 1) * 512, :], in_=hw_[m2 * 512:(m2 + 1) * 512, :])),
                       (f"{pn}{m2}", bump(f"{pn}{m2}", 16) and (f"{pn}{m2}", 16))[1])
            if kind in ("scH1", "scH2"):
                if (kind, ms, "t") not in kind_waited2:
                    kind_waited2.add((kind, ms, "t"))
                    pn = "s_pH1p" if kind == "scH1" else "s_pH2p"
                    W("sync", f"{pn}{ms}", 16)
                    W("sync", f"{pn}{ms + 1}", 16)
            elif kind not in kind_waited2:
                kind_waited2.add(kind)
                W("sync", *prep_marks[kind])
            if t_i >= NSG:
                W("sync", f"s_sgw{sl_}", 16 * (t_i // NSG))
            OP("sync", (lambda E, srcT=srcT, ms=ms, kc=kc, sl_=sl_:
                E.dma_start(out=stg[:, sl_ * 1024:(sl_ + 1) * 1024],
                            in_=srcT[ms * 512:(ms + 2) * 512, kc * 128:(kc + 1) * 128],
                            transpose=True)),
               (f"s_sgt{sl_}", bump(f"s_sgt{sl_}", 16) and (f"s_sgt{sl_}", 16))[1])
            W("gpsimd", f"s_sgt{sl_}", CNT[f"s_sgt{sl_}"])
            OP("gpsimd", (lambda E, dst=dst, ms=ms, kc=kc, sl_=sl_:
                E.dma_start(out=dst[ms:ms + 2, kc].rearrange("s p m -> p s m"),
                            in_=stg[:, sl_ * 1024:(sl_ + 1) * 1024]
                                .rearrange("p (s m) -> p s m", s=2))),
               (f"s_sgw{sl_}", bump(f"s_sgw{sl_}", 16) and (f"s_sgw{sl_}", 16))[1])
            nw[sl_] += 1
            sg_gate[(kind, ms)] = tuple(16 * x for x in nw)
            yield

    slab_sched = [emit_slabs(0)]
    for _ in gen_A(0):
        pass
    for _ in gen_scan(0):
        pass
    for c in range(NCH):
        sWg, sWo, sffn, sh1, sh2, sw3 = slab_sched[c]
        K_x3fm = emit_C(c, sWg, sWo)
        if c + 1 < NCH:
            slab_sched.append(emit_slabs(c + 1))
            a_gen = gen_A(c + 1)
            s_gen = gen_scan(c + 1)
        else:
            a_gen = s_gen = None
        d_gen = gen_D(c, K_x3fm, sffn, sh1, sh2, sw3)
        i = 0
        a_done = s_done = a_gen is None
        for _ in d_gen:
            i += 1
            if not a_done and i % 2 == 0:
                try:
                    next(a_gen)
                except StopIteration:
                    a_done = True
            elif not s_done and a_done and i % 2 == 0:
                try:
                    next(s_gen)
                except StopIteration:
                    s_done = True
        while not a_done:
            try:
                next(a_gen)
            except StopIteration:
                a_done = True
        while not s_done:
            try:
                next(s_gen)
            except StopIteration:
                s_done = True

    W("gpsimd", "s_st", CNT["s_st"])

    final = dict(CNT)
    for e in ENGINES:
        for item in plan[e]:
            if item[0] == "w":
                assert item[2] <= final[item[1]], (e, item)

    with nc.Block() as block:
        def replay(eng_name):
            def f(E):
                for item in plan[eng_name]:
                    if item[0] == "w":
                        E.wait_ge(SEM[item[1]], item[2])
                    else:
                        ins = item[1](E)
                        if item[2] is not None:
                            ins.then_inc(SEM[item[2][0]], item[2][1])
            return f
        block.tensor(replay("tensor"))
        block.scalar(replay("scalar"))
        block.vector(replay("vector"))
        block.sync(replay("sync"))
        block.gpsimd(replay("gpsimd"))

    es.close()
    return nc


def kernel(**inputs):
    nc = build()
    identf = np.eye(128, dtype=np.float32)
    base = {
        "W_in": inputs["L0_attn_W_in"], "b_in": inputs["L0_attn_b_in"],
        "W_gate": inputs["L0_attn_W_gate"], "b_gate": inputs["L0_attn_b_gate"],
        "A": inputs["L0_attn_A"], "Bm": inputs["L0_attn_Bm"],
        "Cw": inputs["L0_attn_C"], "Dw": inputs["L0_attn_D"],
        "W_out": inputs["L0_attn_W_out"], "b_out": inputs["L0_attn_b_out"],
        "w1w": inputs["L0_ffn_w1_w"], "w1s": inputs["L0_ffn_w1_s"],
        "w2w": inputs["L0_ffn_w2_w"], "w2s": inputs["L0_ffn_w2_s"],
        "h1w": inputs["L0_ffn_h1_w"], "h1s": inputs["L0_ffn_h1_s"],
        "h2w": inputs["L0_ffn_h2_w"], "h2s": inputs["L0_ffn_h2_s"],
        "w3w": inputs["L0_ffn_w3_w"], "w3s": inputs["L0_ffn_w3_s"],
        "identf": identf,
    }
    base = {k: np.ascontiguousarray(np.asarray(v, dtype=np.float32)) for k, v in base.items()}
    x_full = np.ascontiguousarray(np.asarray(inputs["L0_attn_x"], dtype=np.float32))
    in_maps = []
    for i in range(N_CORES):
        m = dict(base)
        m["x"] = x_full[2 * i: 2 * i + 2]
        in_maps.append(m)
    res = run_bass_kernel_spmd(nc, in_maps, core_ids=list(range(N_CORES)))
    out = np.empty((16, T, D), dtype=np.float32)
    for i in range(N_CORES):
        out[2 * i: 2 * i + 2] = np.asarray(res.results[i]["y"], dtype=np.float32)
    return out


# version 14
# speedup vs baseline: 1.0277x; 1.0277x over previous
import contextlib
import numpy as np
import ml_dtypes
import concourse.bass as bass
import concourse.mybir as mybir
from concourse.bass_utils import run_bass_kernel_spmd

F32 = mybir.dt.float32
BF16 = mybir.dt.bfloat16
AF = mybir.ActivationFunctionType
ALU = mybir.AluOpType
AX = mybir.AxisListType

N_CORES = 8
D = 1024
DC = 8
S = 256
H = 4096
HC = 32
T = 2048
B = 2
NCH = 8
TC = 256
NT = 512
GRP = 32
NSLOT = 12
EPS = 1e-5
ENGINES = ("tensor", "scalar", "vector", "sync", "gpsimd")


def build():
    nc = bass.Bass("TRN2", target_bir_lowering=False)

    x_d = nc.dram_tensor("x", [B, T, D], F32, kind="ExternalInput").ap()
    W_in = nc.dram_tensor("W_in", [D, S], F32, kind="ExternalInput").ap()
    b_in = nc.dram_tensor("b_in", [S], F32, kind="ExternalInput").ap()
    W_gate = nc.dram_tensor("W_gate", [D, D], F32, kind="ExternalInput").ap()
    b_gate = nc.dram_tensor("b_gate", [D], F32, kind="ExternalInput").ap()
    A_w = nc.dram_tensor("A", [S, S], F32, kind="ExternalInput").ap()
    Bm_w = nc.dram_tensor("Bm", [S, S], F32, kind="ExternalInput").ap()
    C_w = nc.dram_tensor("Cw", [D, S], F32, kind="ExternalInput").ap()
    D_w = nc.dram_tensor("Dw", [D, S], F32, kind="ExternalInput").ap()
    W_out = nc.dram_tensor("W_out", [D, D], F32, kind="ExternalInput").ap()
    b_out = nc.dram_tensor("b_out", [D], F32, kind="ExternalInput").ap()
    w1w = nc.dram_tensor("w1w", [H, D], F32, kind="ExternalInput").ap()
    w1s = nc.dram_tensor("w1s", [H], F32, kind="ExternalInput").ap()
    w2w = nc.dram_tensor("w2w", [H, D], F32, kind="ExternalInput").ap()
    w2s = nc.dram_tensor("w2s", [H], F32, kind="ExternalInput").ap()
    h1w = nc.dram_tensor("h1w", [H, H], F32, kind="ExternalInput").ap()
    h1s = nc.dram_tensor("h1s", [H], F32, kind="ExternalInput").ap()
    h2w = nc.dram_tensor("h2w", [H, H], F32, kind="ExternalInput").ap()
    h2s = nc.dram_tensor("h2s", [H], F32, kind="ExternalInput").ap()
    w3w = nc.dram_tensor("w3w", [D, H], F32, kind="ExternalInput").ap()
    w3s = nc.dram_tensor("w3s", [D], F32, kind="ExternalInput").ap()
    identf_d = nc.dram_tensor("identf", [128, 128], F32, kind="ExternalInput").ap()
    y_d = nc.dram_tensor("y", [B, T, D], F32, kind="ExternalOutput").ap()
    scWg = nc.dram_tensor("scWg", [D, D], BF16, kind="Internal").ap()
    scWo = nc.dram_tensor("scWo", [D, D], BF16, kind="Internal").ap()
    scW1 = nc.dram_tensor("scW1", [H, D], BF16, kind="Internal").ap()
    scW2 = nc.dram_tensor("scW2", [H, D], BF16, kind="Internal").ap()
    scH1 = nc.dram_tensor("scH1", [H, H], BF16, kind="Internal").ap()
    scH2 = nc.dram_tensor("scH2", [H, H], BF16, kind="Internal").ap()
    scW3 = nc.dram_tensor("scW3", [D, H], BF16, kind="Internal").ap()
    scC = nc.dram_tensor("scC", [D, S], BF16, kind="Internal").ap()
    scD = nc.dram_tensor("scD", [D, S], BF16, kind="Internal").ap()
    sc2w1 = nc.dram_tensor("sc2w1", [8, 8, 128, 512], BF16, kind="Internal").ap()
    sc2w2 = nc.dram_tensor("sc2w2", [8, 8, 128, 512], BF16, kind="Internal").ap()
    sc2h1 = nc.dram_tensor("sc2h1", [8, 32, 128, 512], BF16, kind="Internal").ap()
    sc2h2 = nc.dram_tensor("sc2h2", [8, 32, 128, 512], BF16, kind="Internal").ap()
    sc2w3 = nc.dram_tensor("sc2w3", [2, 32, 128, 512], BF16, kind="Internal").ap()

    es = contextlib.ExitStack()
    sb = lambda name, shape, dt: es.enter_context(nc.sbuf_tensor(name, shape, dt))
    ps = lambda name, shape, dt: es.enter_context(nc.psum_tensor(name, shape, dt))

    wIn_sb = sb("wIn_sb", [128, 2048], BF16)
    A_sb = sb("A_sb", [128, 512], BF16)
    Bm_sb = sb("Bm_sb", [128, 512], BF16)
    CT_sb = sb("CT_sb", [128, 2048], BF16)
    DT_sb = sb("DT_sb", [128, 2048], BF16)
    idf_sb = sb("idf_sb", [128, 128], F32)
    bin_sb = sb("bin_sb", [128, 2], F32)
    bgate_sb = sb("bgate_sb", [128, 8], F32)
    bout_sb = sb("bout_sb", [128, 8], F32)
    w1s_sb = sb("w1s_sb", [128, 32], F32)
    w2s_sb = sb("w2s_sb", [128, 32], F32)
    h1s_sb = sb("h1s_sb", [128, 32], F32)
    h2s_sb = sb("h2s_sb", [128, 32], F32)
    h1se_sb = sb("h1se_sb", [128, 32], F32)
    h2se_sb = sb("h2se_sb", [128, 32], F32)
    w3s_sb = sb("w3s_sb", [128, 8], F32)
    x_tm = sb("x_tm", [128, 4096], F32)
    x2_tm = sb("x2_tm", [128, 4096], F32)
    tm1 = sb("tm1", [128, 4096], F32)
    tm2 = sb("tm2", [128, 4096], F32)
    x1_fm = sb("x1_fm", [128, 4096], BF16)
    ysx3_fm = sb("ysx3_fm", [128, 4096], BF16)
    g_fm = sb("g_fm", [128, 4096], BF16)
    uT = sb("uT", [128, 1024], BF16)
    hT = [sb("hT0", [128, 1024], BF16), sb("hT1", [128, 1024], BF16)]
    h_buf = sb("h_buf", [128, 16384], BF16)
    g1_buf = sb("g1_buf", [128, 16384], BF16)
    ring = sb("ring", [128, 512 * NSLOT], BF16)
    tmp = sb("tmp", [128, 2048], F32)
    stage = sb("stage", [128, 1024], F32)
    mk = sb("mk", [128, 4], F32)
    stg = sb("stg", [128, 4096], BF16)
    stat = sb("stat", [128, 56], F32)

    pf = [ps(f"pf{i}", [128, 512], F32) for i in range(4)]
    pscan = ps("pscan", [128, 512], F32)
    pab = [ps("pa", [128, 512], F32), ps("pb", [128, 512], F32)]
    ptp = ps("ptp", [128, 512], F32)

    sem_names = (["s_x", "s_st", "s_ct", "s_slabf"]
                 + [f"s_slot{i}" for i in range(NSLOT)]
                 + ["s_prepmisc", "s_pWg", "s_pC", "s_pD", "s_pW1", "s_pW2",
                    "s_pWo", "s_pH1", "s_pH2", "s_pW3",
                    "s_sgt0", "s_sgt1", "s_sgt2", "s_sgt3",
                    "s_sgw0", "s_sgw1", "s_sgw2", "s_sgw3",
                    "s_pH1p0", "s_pH1p1", "s_pH1p2", "s_pH1p3",
                    "s_pH1p4", "s_pH1p5", "s_pH1p6", "s_pH1p7",
                    "s_pH2p0", "s_pH2p1", "s_pH2p2", "s_pH2p3",
                    "s_pH2p4", "s_pH2p5", "s_pH2p6", "s_pH2p7",
                    "scanp", "scanh", "pa_r", "pa_f", "pb_r", "pb_f",
                    "pf0_r", "pf0_f", "pf1_r", "pf1_f", "pf2_r", "pf2_f",
                    "pf3_r", "pf3_f", "ptp_r", "ptp_f", "dvx", "acx", "vs"])
    SEM = {n: es.enter_context(nc.semaphore(n)) for n in sem_names}
    CNT = {n: 0 for n in sem_names}

    plan = {e: [] for e in ENGINES}
    pend = {e: None for e in ENGINES}

    def W(eng, sem, val):
        if val <= 0:
            return
        assert val <= 10**9
        plan[eng].append(("w", sem, val))

    def OP(eng, builder, inc=None):
        p = pend[eng]
        if inc is None and p is not None:
            inc = p
            pend[eng] = None
        plan[eng].append(("o", builder, inc))

    def bump(sem, amt=1):
        CNT[sem] += amt
        return CNT[sem]

    def defer_inc(eng, sem, amt):
        assert pend[eng] is None
        pend[eng] = (sem, bump(sem, amt) and (sem, amt))[1]

    def emit_xload(c):
        for tt in range(4):
            b, off = tt // 2, (tt % 2) * 128
            OP("scalar", (lambda E, b=b, off=off, c=c:
                E.dma_start(out=x_tm[:, (b * 2 + off // 128) * 1024:(b * 2 + off // 128 + 1) * 1024],
                            in_=x_d[b, c * TC + off: c * TC + off + 128, :])),
               ("s_x", bump("s_x", 16) and ("s_x", 16))[1])

    emit_xload(0)

    prep_marks = {}

    def prep_dma(builder, mark=None, sem="s_prepmisc"):
        OP("gpsimd", builder, (sem, bump(sem, 16) and (sem, 16))[1])
        if mark:
            prep_marks[mark] = (sem, CNT[sem])

    prep_dma(lambda E: E.dma_start(out=idf_sb[:], in_=identf_d[:]))
    prep_dma(lambda E: E.dma_start(
        out=wIn_sb[:].rearrange("p (kc m) -> p kc m", kc=DC),
        in_=W_in.rearrange("(kc p) o -> p kc o", p=128)), mark="wIn")
    prep_dma(lambda E: E.dma_start(
        out=A_sb[:].rearrange("p (si m) -> p si m", si=2),
        in_=A_w.rearrange("(si p) o -> p si o", p=128)), mark="A")
    prep_dma(lambda E: E.dma_start(
        out=Bm_sb[:].rearrange("p (si m) -> p si m", si=2),
        in_=Bm_w.rearrange("(si p) o -> p si o", p=128)), mark="Bm")
    stage_specs = [("b_in", b_in, 2, S), ("b_gate", b_gate, 8, D), ("b_out", b_out, 8, D),
                   ("w1s", w1s, 32, H), ("w2s", w2s, 32, H),
                   ("h1s", h1s, 32, H), ("h2s", h2s, 32, H), ("w3s", w3s, 8, D)]
    for j, (nm, src, k, n) in enumerate(stage_specs):
        prep_dma(lambda E, src=src, k=k, j=j: E.dma_start(
            out=tmp[0:k, j * 128:(j + 1) * 128],
            in_=src.rearrange("(k p) -> k p", p=128)), mark=nm)
    KIND_SEM = {"scWg": "s_pWg", "scC": "s_pC", "scD": "s_pD", "scW1": "s_pW1",
                "scW2": "s_pW2", "scWo": "s_pWo", "scH1": "s_pH1", "scH2": "s_pH2",
                "scW3": "s_pW3"}

    def cast_parts(dst, src, rows, cols, parts, mark):
        step = rows // parts
        for i in range(parts):
            prep_dma(lambda E, i=i, dst=dst, src=src, step=step:
                     E.dma_start(out=dst[i * step:(i + 1) * step, :],
                                 in_=src[i * step:(i + 1) * step, :]),
                     mark=mark if i == parts - 1 else None, sem=KIND_SEM[mark])

    cast_parts(scWg, W_gate, D, D, 2, "scWg")
    cast_parts(scC, C_w, D, S, 1, "scC")
    cast_parts(scD, D_w, D, S, 1, "scD")
    cast_parts(scW2, w2w, H, D, 4, "scW2")
    cast_parts(scW1, w1w, H, D, 4, "scW1")
    cast_parts(scWo, W_out, D, D, 2, "scWo")
    cast_parts(scW3, w3w, D, H, 4, "scW3")

    W("sync", *prep_marks["scC"])
    for sc in range(2):
        OP("sync", lambda E, sc=sc: E.dma_start(
            out=CT_sb[:, sc * 1024:(sc + 1) * 1024],
            in_=scC[:, sc * 128:(sc + 1) * 128], transpose=True),
           ("s_ct", bump("s_ct", 16) and ("s_ct", 16))[1])
    W("sync", *prep_marks["scD"])
    for sc in range(2):
        OP("sync", lambda E, sc=sc: E.dma_start(
            out=DT_sb[:, sc * 1024:(sc + 1) * 1024],
            in_=scD[:, sc * 128:(sc + 1) * 128], transpose=True),
           ("s_ct", bump("s_ct", 16) and ("s_ct", 16))[1])

    stage_plan = []
    for kind, srcT, dst, ngrp, nkc in [("scW2", scW2, sc2w2, 8, 8), ("scW1", scW1, sc2w1, 8, 8),
                                       ("scW3", scW3, sc2w3, 2, 32),
                                       ("scH1", scH1, sc2h1, 8, 32), ("scH2", scH2, sc2h2, 8, 32)]:
        for ms in range(0, ngrp, 2):
            for kc in range(nkc):
                stage_plan.append((kind, srcT, dst, ms, kc))
    sg_gate = {}

    OP("vector", lambda E: E.memset(hT[0][:], 0.0))
    OP("vector", lambda E: E.memset(hT[1][:], 0.0),
       ("dvx", bump("dvx") and ("dvx", 1))[1])
    K_memset = CNT["dvx"]

    W("tensor", *prep_marks["w3s"])
    offs = []
    off = 0
    for nm, src, k, n in stage_specs:
        offs.append((nm, off, k))
        off += k
    for j, (nm, o, k) in enumerate(offs):
        OP("tensor", (lambda E, j=j, o=o, k=k:
            E.transpose(ptp[:, o:o + k], tmp[0:k, j * 128:(j + 1) * 128], idf_sb[0:k, 0:k])),
           None if j < len(offs) - 1 else ("ptp_r", bump("ptp_r") and ("ptp_r", 1))[1])
    bias_dst = {"b_in": bin_sb, "b_gate": bgate_sb, "b_out": bout_sb,
                "w1s": w1s_sb, "w2s": w2s_sb, "h1s": h1s_sb, "h2s": h2s_sb, "w3s": w3s_sb}
    W("vector", "ptp_r", CNT["ptp_r"])
    for j, (nm, o, k) in enumerate(offs):
        OP("vector", (lambda E, nm=nm, o=o, k=k:
            E.tensor_copy(bias_dst[nm][:], ptp[:, o:o + k])),
           None if j < len(offs) - 1 else ("ptp_f", bump("ptp_f") and ("ptp_f", 1))[1])
    OP("vector", lambda E: E.tensor_scalar_mul(h1se_sb[:], h1s_sb[:], float(1.0 / np.sqrt(2.0))))
    OP("vector", lambda E: E.tensor_scalar_mul(h2se_sb[:], h2s_sb[:], float(1.0 / np.sqrt(2.0))),
       ("vs", bump("vs") and ("vs", 1))[1])
    K_scales = CNT["vs"]
    tp_round_bias_done = True

    slab_idx = [0]
    slab_first_wait = set()

    def slab_load(kind, src, row0, col0, transpose):
        assert not transpose
        j = slab_idx[0]
        slab_idx[0] += 1
        slot = j % NSLOT
        if kind not in slab_first_wait:
            slab_first_wait.add(kind)
            W("sync", *prep_marks[kind])
        if j >= NSLOT:
            W("sync", "s_slabf", j - NSLOT + 1)
        semn = f"s_slot{slot}"
        OP("sync", (lambda E, slot=slot, src=src, row0=row0, col0=col0:
            E.dma_start(out=ring[:, slot * 512:(slot + 1) * 512],
                        in_=src[row0:row0 + 128, col0:col0 + 512])),
           (semn, bump(semn, 16) and (semn, 16))[1])
        return slot, (semn, CNT[semn])

    def slab_load_quad(kind, sc2, ms, kc, nq):
        j = slab_idx[0]
        assert j % 2 == 0 and NSLOT % nq == 0 and (nq == 2 or j % 4 == 0)
        slab_idx[0] += nq
        slot = j % NSLOT
        key = (kind, (ms // 2) * 2)
        if key not in slab_first_wait:
            slab_first_wait.add(key)
            for q_ in range(4):
                W("sync", f"s_sgw{q_}", sg_gate[key][q_])
        if j + nq - 1 >= NSLOT:
            W("sync", "s_slabf", j + nq - NSLOT)
        semn = f"s_slot{slot}"
        OP("sync", (lambda E, slot=slot, sc2=sc2, ms=ms, kc=kc, nq=nq:
            E.dma_start(out=ring[:, slot * 512:(slot + nq) * 512],
                        in_=sc2[ms, kc:kc + nq].rearrange("k p m -> p k m"))),
           (semn, bump(semn, 16) and (semn, 16))[1])
        av = (semn, CNT[semn])
        return [(slot + q_, av) for q_ in range(nq)]

    def slab(slot):
        return ring[:, slot * 512:(slot + 1) * 512]

    bank_uses = [0, 0]
    rr = [0]

    def next_bank():
        i = rr[0] % 2
        rr[0] += 1
        return i

    def te_begin_round(i):
        W("tensor", ("pa_f", "pb_f")[i], bank_uses[i])

    def te_end_round(i):
        return (("pa_r", "pb_r")[i], bump(("pa_r", "pb_r")[i]) and (("pa_r", "pb_r")[i], 1))[1]

    def consumer_wait(eng, i):
        W(eng, ("pa_r", "pb_r")[i], bank_uses[i] + 1)

    def consumer_done(i):
        bank_uses[i] += 1
        return (("pa_f", "pb_f")[i], bump(("pa_f", "pb_f")[i]) and (("pa_f", "pb_f")[i], 1))[1]

    tp_round = [1]

    def emit_transpose_round(srcs, dsts, dve_inc_extra=None, te_wait=None):
        k = tp_round[0]
        tp_round[0] += 1
        if te_wait is not None:
            W("tensor", te_wait[0], te_wait[1])
        W("tensor", "ptp_f", k)
        n = len(srcs)
        for q, s_ in enumerate(srcs):
            OP("tensor", (lambda E, q=q, s_=s_:
                E.transpose(ptp[:, q * 128:(q + 1) * 128], s_, idf_sb[:])),
               None if q < n - 1 else ("ptp_r", bump("ptp_r") and ("ptp_r", 1))[1])
        W("vector", "ptp_r", CNT["ptp_r"])
        for q, d_ in enumerate(dsts):
            last = q == n - 1
            inc = None
            if last:
                inc = ("ptp_f", bump("ptp_f") and ("ptp_f", 1))[1]
            OP("vector", (lambda E, q=q, d_=d_:
                E.tensor_copy(d_, ptp[:, q * 128:(q + 1) * 128])), inc)
            if last and dve_inc_extra is not None:
                pass

    def vs_edge():
        OP("vector", lambda E: E.tensor_copy(mk[:, 0:1], mk[:, 0:1]),
           ("vs", bump("vs") and ("vs", 1))[1])
        W("vector", "vs", CNT["vs"])

    ln_count = [0]

    def ln_chain(g0):
        OP("vector", lambda E, g0=g0: E.tensor_add(stat[:, g0 + 4:g0 + 8], stat[:, g0 + 4:g0 + 8], stat[:, g0 + 8:g0 + 12]))
        OP("vector", lambda E, g0=g0: E.tensor_scalar_mul(stat[:, g0 + 12:g0 + 16], stat[:, g0 + 0:g0 + 4], 1.0 / D))
        vs_edge()
        OP("vector", lambda E, g0=g0: E.tensor_mul(stat[:, g0 + 24:g0 + 28], stat[:, g0 + 12:g0 + 16], stat[:, g0 + 12:g0 + 16]))
        vs_edge()
        OP("vector", lambda E, g0=g0: E.scalar_tensor_tensor(
            stat[:, g0 + 16:g0 + 20], stat[:, g0 + 4:g0 + 8], 1.0 / D, stat[:, g0 + 24:g0 + 28], ALU.mult, ALU.subtract))
        vs_edge()
        OP("vector", lambda E, g0=g0: E.tensor_scalar_add(stat[:, g0 + 16:g0 + 20], stat[:, g0 + 16:g0 + 20], EPS))
        vs_edge()
        I32 = mybir.dt.int32
        OP("vector", lambda E, g0=g0: E.tensor_single_scalar(
            stat[:, g0 + 24:g0 + 28].bitcast(I32), stat[:, g0 + 16:g0 + 20].bitcast(I32),
            1, ALU.logical_shift_right))
        vs_edge()
        OP("vector", lambda E, g0=g0: E.tensor_scalar(
            stat[:, g0 + 20:g0 + 24].bitcast(I32), stat[:, g0 + 24:g0 + 28].bitcast(I32),
            -1, 0x5F3759DF, ALU.mult, ALU.add))
        yield
        for _ in range(3):
            vs_edge()
            OP("vector", lambda E, g0=g0: E.tensor_mul(stat[:, g0 + 24:g0 + 28], stat[:, g0 + 20:g0 + 24], stat[:, g0 + 20:g0 + 24]))
            vs_edge()
            OP("vector", lambda E, g0=g0: E.scalar_tensor_tensor(
                stat[:, g0 + 24:g0 + 28], stat[:, g0 + 24:g0 + 28], -0.5, stat[:, g0 + 16:g0 + 20], ALU.mult, ALU.mult))
            vs_edge()
            OP("vector", lambda E, g0=g0: E.scalar_tensor_tensor(
                stat[:, g0 + 20:g0 + 24], stat[:, g0 + 24:g0 + 28], 1.5, stat[:, g0 + 20:g0 + 24], ALU.add, ALU.mult))
            yield
        vs_edge()

    def emit_ln(src, dst_write, act_wait=None, dve_wait=None):
        g0 = (ln_count[0] % 2) * 28
        ln_count[0] += 1
        if act_wait is not None:
            W("scalar", act_wait[0], act_wait[1])
        for tt in range(4):
            for hf in range(2):
                last = tt == 3 and hf == 1
                col = g0 + (4 if hf == 0 else 8) + tt
                OP("scalar", (lambda E, tt=tt, hf=hf, col=col:
                    E.activation(out=stage[:, hf * 512:(hf + 1) * 512],
                                 in_=src[:, tt * 1024 + hf * 512: tt * 1024 + (hf + 1) * 512],
                                 func=AF.Square,
                                 accum_out=stat[:, col:col + 1])),
                   ("acx", bump("acx") and ("acx", 1))[1] if last else None)
        K_sq = CNT["acx"]
        if dve_wait is not None:
            W("vector", dve_wait[0], dve_wait[1])
        for tt in range(4):
            OP("vector", (lambda E, tt=tt, g0=g0:
                E.reduce_sum(stat[:, g0 + tt:g0 + tt + 1], src[:, tt * 1024:(tt + 1) * 1024], axis=AX.X)))
        W("vector", "acx", K_sq)
        for _ in ln_chain(g0):
            pass
        for tt in range(4):
            dst_write(tt, g0)

    def ln_apply(dst, src, tt, g0, inc=None):
        OP("vector", (lambda E, tt=tt, g0=g0:
            E.tensor_scalar(dst[:, tt * 1024:(tt + 1) * 1024], src[:, tt * 1024:(tt + 1) * 1024],
                            stat[:, g0 + 12 + tt:g0 + 13 + tt], stat[:, g0 + 20 + tt:g0 + 21 + tt],
                            ALU.subtract, ALU.mult)), inc)

    pf_uses = [0, 0, 0, 0]
    K_sq_ln4 = [0]
    u_marks = {}

    def emit_xload_gated(c, gate):
        if gate is not None:
            W("scalar", gate[0], gate[1])
        emit_xload(c)

    def emit_slabs(c):
        slabs_Wg = [[slab_load("scWg", scWg, kc * 128, h * 512, False)
                     for kc in range(8)] for h in range(2)]
        slabs_Wo = [[slab_load("scWo", scWo, kc * 128, h * 512, False)
                     for kc in range(8)] for h in range(2)]
        stg_gen = emit_staging() if c == 0 else None

        def pump_stage(n):
            if stg_gen is None:
                return
            for _ in range(n):
                try:
                    next(stg_gen)
                except StopIteration:
                    break

        def pair_group(kind, sc2, ms, nkc, nq=4):
            out = []
            for kc in range(0, nkc, nq):
                out.extend(slab_load_quad(kind, sc2, ms, kc, nq))
            return out
        pump_stage(64)
        slabs_ffn = []
        for ms in range(8):
            w2sl = pair_group("scW2", sc2w2, ms, 8, nq=2)
            w1sl = pair_group("scW1", sc2w1, ms, 8, nq=2)
            slabs_ffn.append((w2sl, w1sl))
        pump_stage(10 ** 9)
        slabs_h1 = [pair_group("scH1", sc2h1, ms, 32) for ms in range(8)]
        slabs_h2 = [pair_group("scH2", sc2h2, ms, 32) for ms in range(8)]
        slabs_w3 = [pair_group("scW3", sc2w3, ms, 32) for ms in range(2)]
        return slabs_Wg, slabs_Wo, slabs_ffn, slabs_h1, slabs_h2, slabs_w3

    def gen_A(c):
        g0 = (ln_count[0] % 2) * 28
        ln_count[0] += 1
        xg = ("s_x", 64 * (c + 1))
        W("scalar", xg[0], xg[1])
        for tt in range(4):
            for hf in range(2):
                last = tt == 3 and hf == 1
                col = g0 + (4 if hf == 0 else 8) + tt
                OP("scalar", (lambda E, tt=tt, hf=hf, col=col:
                    E.activation(out=stage[:, hf * 512:(hf + 1) * 512],
                                 in_=x_tm[:, tt * 1024 + hf * 512: tt * 1024 + (hf + 1) * 512],
                                 func=AF.Square, accum_out=stat[:, col:col + 1])),
                   ("acx", bump("acx") and ("acx", 1))[1] if last else None)
        K_sq = CNT["acx"]
        yield
        W("vector", xg[0], xg[1])
        for tt in range(4):
            OP("vector", (lambda E, tt=tt, g0=g0:
                E.reduce_sum(stat[:, g0 + tt:g0 + tt + 1], x_tm[:, tt * 1024:(tt + 1) * 1024], axis=AX.X)))
        W("vector", "acx", K_sq)
        yield
        yield from ln_chain(g0)
        for tt in range(4):
            ln_apply(tm1, x_tm, tt, g0,
                     inc=("dvx", bump("dvx") and ("dvx", 1))[1] if tt == 3 else None)
        K_ln1 = CNT["dvx"]
        yield
        for dc in range(DC):
            emit_transpose_round(
                [tm1[:, tt * 1024 + dc * 128: tt * 1024 + dc * 128 + 128] for tt in range(4)],
                [x1_fm[:, dc * 512 + tt * 128: dc * 512 + (tt + 1) * 128] for tt in range(4)],
                te_wait=("dvx", K_ln1) if dc == 0 else None)
            yield
        K_x1fm = CNT["ptp_f"]
        marks = []
        W("tensor", "ptp_f", K_x1fm)
        if c == 0:
            W("tensor", *prep_marks["w3s"])
        for sc in range(2):
            i = next_bank()
            te_begin_round(i)
            for kc in range(DC):
                OP("tensor", (lambda E, i=i, kc=kc, sc=sc:
                    E.matmul(pab[i][:], wIn_sb[:, kc * 256 + sc * 128: kc * 256 + (sc + 1) * 128],
                             x1_fm[:, kc * 512:(kc + 1) * 512],
                             start=(kc == 0), stop=(kc == DC - 1))),
                   te_end_round(i) if kc == DC - 1 else None)
            consumer_wait("scalar", i)
            if c == 0 and sc == 0:
                W("scalar", "ptp_f", 1)
            OP("scalar", (lambda E, i=i, sc=sc:
                E.activation(out=uT[:, sc * 512:(sc + 1) * 512], in_=pab[i][:],
                             func=AF.Identity, bias=bin_sb[:, sc:sc + 1])),
               consumer_done(i))
            marks.append((("pa_f", "pb_f")[i], bank_uses[i]))
            yield
        u_marks[c] = marks

    def gen_scan(c):
        hcur, hprv = hT[c % 2], hT[(c + 1) % 2]
        if c == 0:
            W("tensor", "dvx", K_memset)
            W("tensor", *prep_marks["w3s"])
        W("tensor", u_marks[c][0][0], u_marks[c][0][1])
        W("tensor", u_marks[c][1][0], u_marks[c][1][1])
        base = c * TC
        for g_i in range(TC // GRP):
            gp = g_i % 2
            t0 = g_i * GRP
            W("tensor", "scanh", 2 * (base + t0))
            first = True
            for so in range(2):
                for b in range(2):
                    for si in range(2):
                        OP("tensor", (lambda E, gp=gp, so=so, b=b, si=si, t0=t0, first=first:
                            E.matmul(pscan[:, gp * 256 + so * 64 + b * 32: gp * 256 + so * 64 + (b + 1) * 32],
                                     Bm_sb[:, (si * 2 + so) * 128:(si * 2 + so + 1) * 128],
                                     uT[:, si * 512 + b * 256 + t0: si * 512 + b * 256 + t0 + GRP],
                                     start=first, stop=(so == 1 and b == 1 and si == 1),
                                     skip_group_check=True)))
                        first = False
            yield
            for dt_ in range(GRP):
                tt = t0 + dt_
                gt = base + tt
                if gt > 0:
                    W("tensor", "scanh", 2 * gt)
                pv = pscan[:, gp * 256: gp * 256 + 128].rearrange("p (so b t) -> p so b t", so=2, b=2)
                for so in range(2):
                    for si in range(2):
                        if tt == 0:
                            rhsb = (lambda E, si=si, hp=hprv:
                                    hp[:].rearrange("p (sc b t) -> p sc b t", sc=2, b=2)[:, si, :, TC - 1])
                        else:
                            rhsb = (lambda E, si=si, hc_=hcur, tt=tt:
                                    hc_[:].rearrange("p (sc b t) -> p sc b t", sc=2, b=2)[:, si, :, tt - 1])
                        OP("tensor", (lambda E, pv=pv, so=so, si=si, dt_=dt_, rhsb=rhsb:
                            E.matmul(pv[:, so, :, dt_],
                                     A_sb[:, (si * 2 + so) * 128:(si * 2 + so + 1) * 128],
                                     rhsb(E), start=False, stop=(si == 1),
                                     skip_group_check=True)),
                           ("scanp", bump("scanp") and ("scanp", 1))[1] if (so == 1 and si == 1) else None)
                W("scalar", "scanp", CNT["scanp"])
                OP("scalar", (lambda E, gp=gp, dt_=dt_, tt=tt, hc_=hcur:
                    E.activation(
                        out=hc_[:].rearrange("p (sc b t) -> p sc b t", sc=2, b=2)[:, :, :, tt],
                        in_=pscan[:, gp * 256: gp * 256 + 128]
                            .rearrange("p (so b t) -> p so b t", so=2, b=2)[:, :, :, dt_],
                        func=AF.Tanh)),
                   ("scanh", bump("scanh", 2) and ("scanh", 2))[1])
                yield

    def emit_C(c, slabs_Wg, slabs_Wo):
        hcur = hT[c % 2]
        if c == 0:
            W("tensor", "s_ct", 64)
        W("tensor", "scanh", 2 * TC * (c + 1))

        def g_round(dc):
            h = dc // 4
            i = next_bank()
            te_begin_round(i)
            for kc in range(DC):
                slot, avail = slabs_Wg[h][kc]
                if dc % 4 == 0:
                    W("tensor", avail[0], avail[1])
                inc = None
                if kc == DC - 1:
                    inc = te_end_round(i)
                elif dc % 4 == 3:
                    inc = ("s_slabf", bump("s_slabf") and ("s_slabf", 1))[1]
                OP("tensor", (lambda E, i=i, slot=slot, dc=dc, kc=kc:
                    E.matmul(pab[i][:], slab(slot)[:, (dc % 4) * 128:(dc % 4 + 1) * 128],
                             x1_fm[:, kc * 512:(kc + 1) * 512],
                             start=(kc == 0), stop=(kc == DC - 1))), inc)
            if dc % 4 == 3:
                defer_inc("tensor", "s_slabf", 1)
            consumer_wait("scalar", i)
            OP("scalar", (lambda E, i=i, dc=dc:
                E.activation(out=g_fm[:, dc * 512:(dc + 1) * 512], in_=pab[i][:],
                             func=AF.Sigmoid, bias=bgate_sb[:, dc:dc + 1])),
               consumer_done(i))

        def y_pair(dp):
            banks = []
            for do in (2 * dp, 2 * dp + 1):
                i = next_bank()
                banks.append((do, i))
                te_begin_round(i)
                for j in range(4):
                    mat, src_, sc = ((CT_sb, hcur, 0), (CT_sb, hcur, 1),
                                     (DT_sb, uT, 0), (DT_sb, uT, 1))[j]
                    OP("tensor", (lambda E, i=i, mat=mat, src_=src_, sc=sc, do=do:
                        E.matmul(pab[i][:], mat[:, sc * 1024 + do * 128: sc * 1024 + (do + 1) * 128],
                                 src_[:, sc * 512:(sc + 1) * 512],
                                 start=(mat is CT_sb and sc == 0), stop=(mat is DT_sb and sc == 1))),
                       te_end_round(i) if j == 3 else None)
            for do, i in banks:
                consumer_wait("vector", i)
                sl = 0 if do % 2 == 0 else 1
                OP("vector", (lambda E, i=i, do=do, sl=sl:
                    E.scalar_tensor_tensor(tmp[:, sl * 512:(sl + 1) * 512], pab[i][:], 1.0,
                                           x1_fm[:, do * 512:(do + 1) * 512],
                                           ALU.mult, ALU.subtract)),
                   consumer_done(i))
            vs_edge()
            for do, i in banks:
                sl = 0 if do % 2 == 0 else 1
                OP("vector", (lambda E, do=do, sl=sl:
                    E.tensor_mul(tmp[:, (2 + sl) * 512:(3 + sl) * 512],
                                 tmp[:, sl * 512:(sl + 1) * 512],
                                 g_fm[:, do * 512:(do + 1) * 512])))
            vs_edge()
            for do, i in banks:
                sl = 0 if do % 2 == 0 else 1
                last = dp == 3 and do == 2 * dp + 1
                inc = ("dvx", bump("dvx") and ("dvx", 1))[1] if last else None
                OP("vector", (lambda E, do=do, sl=sl:
                    E.tensor_add(ysx3_fm[:, do * 512:(do + 1) * 512],
                                 tmp[:, (2 + sl) * 512:(3 + sl) * 512],
                                 x1_fm[:, do * 512:(do + 1) * 512])), inc)

        g_round(0); g_round(1); g_round(2); g_round(3)
        y_pair(0)
        g_round(4); g_round(5)
        y_pair(1)
        g_round(6); g_round(7)
        y_pair(2); y_pair(3)
        K_ys = CNT["dvx"]

        W("tensor", "dvx", K_ys)
        first_stt = [True]
        for do in range(DC):
            h = do // 4
            i = next_bank()
            te_begin_round(i)
            for kc in range(DC):
                slot, avail = slabs_Wo[h][kc]
                if do % 4 == 0:
                    W("tensor", avail[0], avail[1])
                inc = None
                if kc == DC - 1:
                    inc = te_end_round(i)
                elif do % 4 == 3:
                    inc = ("s_slabf", bump("s_slabf") and ("s_slabf", 1))[1]
                OP("tensor", (lambda E, i=i, slot=slot, do=do, kc=kc:
                    E.matmul(pab[i][:], slab(slot)[:, (do % 4) * 128:(do % 4 + 1) * 128],
                             ysx3_fm[:, kc * 512:(kc + 1) * 512],
                             start=(kc == 0), stop=(kc == DC - 1))), inc)
            if do % 4 == 3:
                defer_inc("tensor", "s_slabf", 1)
            consumer_wait("vector", i)
            if first_stt[0]:
                first_stt[0] = False
                if K_sq_ln4[0]:
                    W("vector", "acx", K_sq_ln4[0])
            if do >= 2:
                W("vector", "ptp_r", tp_round[0])
            OP("vector", (lambda E, i=i, do=do:
                E.scalar_tensor_tensor(stage[:, (do % 2) * 512:(do % 2 + 1) * 512],
                                       pab[i][:], bout_sb[:, do:do + 1],
                                       x1_fm[:, do * 512:(do + 1) * 512],
                                       ALU.add, ALU.add)),
               consumer_done(i))
            emit_transpose_round(
                [stage[:, (do % 2) * 512 + tt * 128: (do % 2) * 512 + (tt + 1) * 128] for tt in range(4)],
                [tm2[:, tt * 1024 + do * 128: tt * 1024 + (do + 1) * 128] for tt in range(4)],
                te_wait=(("pa_f", "pb_f")[i], bank_uses[i]))
        K_mixtm = CNT["ptp_f"]

        def ln2_write(tt, g0):
            if tt == 0 and c >= 1:
                W("vector", "s_st", 64 * c)
            ln_apply(x2_tm, tm2, tt, g0)
            if tt == 3:
                vs_edge()
                for t2_ in range(4):
                    OP("vector", (lambda E, t2_=t2_:
                        E.tensor_add(x2_tm[:, t2_ * 1024:(t2_ + 1) * 1024],
                                     x2_tm[:, t2_ * 1024:(t2_ + 1) * 1024],
                                     x_tm[:, t2_ * 1024:(t2_ + 1) * 1024])),
                       ("dvx", bump("dvx") and ("dvx", 1))[1] if t2_ == 3 else None)
        emit_ln(tm2, ln2_write, act_wait=("ptp_f", K_mixtm), dve_wait=("ptp_f", K_mixtm))
        K_x2 = CNT["dvx"]
        if c + 1 < NCH:
            emit_xload_gated(c + 1, ("dvx", K_x2))

        emit_ln(x2_tm, lambda tt, g0: ln_apply(tm1, x2_tm, tt, g0,
                inc=("dvx", bump("dvx") and ("dvx", 1))[1] if tt == 3 else None),
                act_wait=("dvx", K_x2), dve_wait=("dvx", K_x2))
        K_ln3 = CNT["dvx"]

        for dc in range(DC):
            emit_transpose_round(
                [tm1[:, tt * 1024 + dc * 128: tt * 1024 + dc * 128 + 128] for tt in range(4)],
                [ysx3_fm[:, dc * 512 + tt * 128: dc * 512 + (tt + 1) * 128] for tt in range(4)],
                te_wait=("dvx", K_ln3) if dc == 0 else None)
        return CNT["ptp_f"]

    def gen_D(c, K_x3fm, slabs_ffn, slabs_h1, slabs_h2, slabs_w3):
        def w12_round(go, bk, sl8, i_rel):
            W("tensor", f"pf{bk}_f", pf_uses[bk])
            for kc in range(8):
                slot, avail = sl8[kc]
                if go == 0:
                    W("tensor", avail[0], avail[1])
                inc = None
                if kc == 7:
                    inc = (f"pf{bk}_r", bump(f"pf{bk}_r") and (f"pf{bk}_r", 1))[1]
                elif i_rel:
                    inc = ("s_slabf", bump("s_slabf") and ("s_slabf", 1))[1]
                OP("tensor", (lambda E, bk=bk, slot=slot, go=go, kc=kc:
                    E.matmul(pf[bk][:], slab(slot)[:, go * 128:(go + 1) * 128],
                             ysx3_fm[:, kc * 512:(kc + 1) * 512],
                             start=(kc == 0), stop=(kc == 7))), inc)
            if i_rel:
                defer_inc("tensor", "s_slabf", 1)

        W("tensor", "ptp_f", K_x3fm)
        if c == 0:
            W("scalar", "vs", K_scales)
        for ms in range(8):
            w2sl, w1sl = slabs_ffn[ms]
            rot = 2 * (ms % 2)
            for go in range(4):
                hc = ms * 4 + go
                bk = (go + rot) % 4
                w12_round(go, bk, w2sl, go == 3)
                W("vector", f"pf{bk}_r", CNT[f"pf{bk}_r"])
                OP("vector", (lambda E, bk=bk, hc=hc:
                    E.tensor_scalar(h_buf[:, hc * 512:(hc + 1) * 512], pf[bk][:],
                                    w2s_sb[:, hc:hc + 1], None, ALU.mult)),
                   (f"pf{bk}_f", bump(f"pf{bk}_f") and (f"pf{bk}_f", 1))[1])
                pf_uses[bk] += 1
                yield
            for go in range(4):
                hc = ms * 4 + go
                bk = (go + rot) % 4
                w12_round(go, bk, w1sl, go == 3)
                W("scalar", f"pf{bk}_r", CNT[f"pf{bk}_r"])
                OP("scalar", (lambda E, bk=bk, hc=hc:
                    E.activation(out=tmp[:, bk * 512:(bk + 1) * 512], in_=pf[bk][:],
                                 func=AF.Sigmoid, scale=w1s_sb[:, hc:hc + 1])),
                   ("acx", bump("acx") and ("acx", 1))[1])
                W("vector", "acx", CNT["acx"])
                OP("vector", (lambda E, bk=bk, hc=hc:
                    E.scalar_tensor_tensor(tmp[:, bk * 512:(bk + 1) * 512], pf[bk][:],
                                           w1s_sb[:, hc:hc + 1], tmp[:, bk * 512:(bk + 1) * 512],
                                           ALU.mult, ALU.mult)))
                vs_edge()
                OP("vector", (lambda E, bk=bk, hc=hc:
                    E.tensor_mul(h_buf[:, hc * 512:(hc + 1) * 512],
                                 h_buf[:, hc * 512:(hc + 1) * 512],
                                 tmp[:, bk * 512:(bk + 1) * 512])),
                   (f"pf{bk}_f", bump(f"pf{bk}_f") and (f"pf{bk}_f", 1))[1])
                pf_uses[bk] += 1
                if hc == 31:
                    OP("vector", lambda E: E.tensor_copy(mk[:, 0:1], mk[:, 0:1]),
                       ("dvx", bump("dvx") and ("dvx", 1))[1])
                yield
        K_h = CNT["dvx"]

        def big_pass(slabs_m, rhs_buf, out_buf, se_sb, s_sb, te_extra_wait):
            for ms in range(8):
                rot = 2 * (ms % 2)
                if te_extra_wait is not None and ms == 0:
                    W("tensor", te_extra_wait[0], te_extra_wait[1])
                for kc in range(32):
                    slot, avail = slabs_m[ms][kc]
                    W("tensor", avail[0], avail[1])
                    for go in range(4):
                        bk = (go + rot) % 4
                        if kc == 0:
                            W("tensor", f"pf{bk}_f", pf_uses[bk])
                        inc = None
                        if kc == 31:
                            inc = (f"pf{bk}_r", bump(f"pf{bk}_r") and (f"pf{bk}_r", 1))[1]
                        elif go == 3:
                            inc = ("s_slabf", bump("s_slabf") and ("s_slabf", 1))[1]
                        OP("tensor", (lambda E, bk=bk, go=go, slot=slot, kc=kc, rhs_buf=rhs_buf:
                            E.matmul(pf[bk][:], slab(slot)[:, go * 128:(go + 1) * 128],
                                     rhs_buf[:, kc * 512:(kc + 1) * 512],
                                     start=(kc == 0), stop=(kc == 31))), inc)
                    yield
                defer_inc("tensor", "s_slabf", 1)
                for go in range(4):
                    hc = ms * 4 + go
                    bk = (go + rot) % 4
                    last = ms == 7 and go == 3
                    W("scalar", f"pf{bk}_r", CNT[f"pf{bk}_r"])
                    OP("scalar", (lambda E, bk=bk, hc=hc, se_sb=se_sb:
                        E.activation(out=tmp[:, bk * 512:(bk + 1) * 512], in_=pf[bk][:],
                                     func=AF.Erf, scale=se_sb[:, hc:hc + 1])),
                       ("acx", bump("acx") and ("acx", 1))[1])
                    W("vector", "acx", CNT["acx"])
                    OP("vector", (lambda E, bk=bk:
                        E.tensor_scalar(tmp[:, bk * 512:(bk + 1) * 512],
                                        tmp[:, bk * 512:(bk + 1) * 512], 1.0, 0.5,
                                        ALU.add, ALU.mult)))
                    vs_edge()
                    OP("vector", (lambda E, bk=bk, hc=hc, s_sb=s_sb, out_buf=out_buf:
                        E.scalar_tensor_tensor(out_buf[:, hc * 512:(hc + 1) * 512], pf[bk][:],
                                               s_sb[:, hc:hc + 1], tmp[:, bk * 512:(bk + 1) * 512],
                                               ALU.mult, ALU.mult)),
                       (f"pf{bk}_f", bump(f"pf{bk}_f") and (f"pf{bk}_f", 1))[1])
                    pf_uses[bk] += 1
                    if last:
                        OP("vector", lambda E: E.tensor_copy(mk[:, 0:1], mk[:, 0:1]),
                           ("dvx", bump("dvx") and ("dvx", 1))[1])
                yield

        yield from big_pass(slabs_h1, h_buf, g1_buf, h1se_sb, h1s_sb, ("dvx", K_h))
        K_g1 = CNT["dvx"]
        yield from big_pass(slabs_h2, g1_buf, h_buf, h2se_sb, h2s_sb, ("dvx", K_g1))
        K_g2 = CNT["dvx"]

        W("tensor", "dvx", K_g2)
        for ms in range(2):
            rot = 2 * (ms % 2)
            for kc in range(32):
                slot, avail = slabs_w3[ms][kc]
                W("tensor", avail[0], avail[1])
                for go in range(4):
                    bk = (go + rot) % 4
                    if kc == 0:
                        W("tensor", f"pf{bk}_f", pf_uses[bk])
                    inc = None
                    if kc == 31:
                        inc = (f"pf{bk}_r", bump(f"pf{bk}_r") and (f"pf{bk}_r", 1))[1]
                    elif go == 3:
                        inc = ("s_slabf", bump("s_slabf") and ("s_slabf", 1))[1]
                    OP("tensor", (lambda E, bk=bk, go=go, slot=slot, kc=kc:
                        E.matmul(pf[bk][:], slab(slot)[:, go * 128:(go + 1) * 128],
                                 h_buf[:, kc * 512:(kc + 1) * 512],
                                 start=(kc == 0), stop=(kc == 31))), inc)
                yield
            defer_inc("tensor", "s_slabf", 1)
            for go in range(4):
                do = ms * 4 + go
                bk = (go + rot) % 4
                W("scalar", f"pf{bk}_r", CNT[f"pf{bk}_r"])
                if do >= 2:
                    W("scalar", "ptp_r", tp_round[0])
                OP("scalar", (lambda E, bk=bk, do=do:
                    E.activation(out=stage[:, (do % 2) * 512:(do % 2 + 1) * 512], in_=pf[bk][:],
                                 func=AF.Identity, scale=w3s_sb[:, do:do + 1])),
                   (f"pf{bk}_f", bump(f"pf{bk}_f") and (f"pf{bk}_f", 1))[1])
                pf_uses[bk] += 1
                emit_transpose_round(
                    [stage[:, (do % 2) * 512 + tt * 128: (do % 2) * 512 + (tt + 1) * 128]
                     for tt in range(4)],
                    [tm2[:, tt * 1024 + do * 128: tt * 1024 + (do + 1) * 128] for tt in range(4)],
                    te_wait=(f"pf{bk}_f", CNT[f"pf{bk}_f"]))
                yield
        K_ffntm = CNT["ptp_f"]

        def ln4_write(tt, g0):
            ln_apply(tm2, tm2, tt, g0)
            if tt == 3:
                vs_edge()
                for t2_ in range(4):
                    OP("vector", (lambda E, t2_=t2_:
                        E.tensor_add(x2_tm[:, t2_ * 1024:(t2_ + 1) * 1024],
                                     x2_tm[:, t2_ * 1024:(t2_ + 1) * 1024],
                                     tm2[:, t2_ * 1024:(t2_ + 1) * 1024])),
                       ("dvx", bump("dvx") and ("dvx", 1))[1] if t2_ == 3 else None)
        emit_ln(tm2, ln4_write, act_wait=("ptp_f", K_ffntm), dve_wait=("ptp_f", K_ffntm))
        K_sq_ln4[0] = CNT["acx"] - 0
        K_xout = CNT["dvx"]

        W("gpsimd", "dvx", K_xout)
        for tt in range(4):
            b, off = tt // 2, (tt % 2) * 128
            OP("gpsimd", (lambda E, b=b, off=off, c=c:
                E.dma_start(out=y_d[b, c * TC + off: c * TC + off + 128, :],
                            in_=x2_tm[:, (b * 2 + off // 128) * 1024:(b * 2 + off // 128 + 1) * 1024])),
               ("s_st", bump("s_st", 16) and ("s_st", 16))[1])

    stage_state = {}

    def emit_staging():
        NSG = 4
        nw = [0] * NSG
        kind_waited2 = set()
        hcast_done = set()
        for t_i, (kind, srcT, dst, ms, kc) in enumerate(stage_plan):
            sl_ = t_i % NSG
            if kind in ("scH1", "scH2") and (kind, ms) not in hcast_done:
                hcast_done.add((kind, ms))
                hw_, scv, pn = (h1w, scH1, "s_pH1p") if kind == "scH1" else (h2w, scH2, "s_pH2p")
                for m2 in (ms, ms + 1):
                    OP("gpsimd", (lambda E, m2=m2, hw_=hw_, scv=scv: E.dma_start(
                        out=scv[m2 * 512:(m2 + 1) * 512, :], in_=hw_[m2 * 512:(m2 + 1) * 512, :])),
                       (f"{pn}{m2}", bump(f"{pn}{m2}", 16) and (f"{pn}{m2}", 16))[1])
            if kind in ("scH1", "scH2"):
                if (kind, ms, "t") not in kind_waited2:
                    kind_waited2.add((kind, ms, "t"))
                    pn = "s_pH1p" if kind == "scH1" else "s_pH2p"
                    W("sync", f"{pn}{ms}", 16)
                    W("sync", f"{pn}{ms + 1}", 16)
            elif kind not in kind_waited2:
                kind_waited2.add(kind)
                W("sync", *prep_marks[kind])
            if t_i >= NSG:
                W("sync", f"s_sgw{sl_}", 16 * (t_i // NSG))
            OP("sync", (lambda E, srcT=srcT, ms=ms, kc=kc, sl_=sl_:
                E.dma_start(out=stg[:, sl_ * 1024:(sl_ + 1) * 1024],
                            in_=srcT[ms * 512:(ms + 2) * 512, kc * 128:(kc + 1) * 128],
                            transpose=True)),
               (f"s_sgt{sl_}", bump(f"s_sgt{sl_}", 16) and (f"s_sgt{sl_}", 16))[1])
            W("gpsimd", f"s_sgt{sl_}", CNT[f"s_sgt{sl_}"])
            OP("gpsimd", (lambda E, dst=dst, ms=ms, kc=kc, sl_=sl_:
                E.dma_start(out=dst[ms:ms + 2, kc].rearrange("s p m -> p s m"),
                            in_=stg[:, sl_ * 1024:(sl_ + 1) * 1024]
                                .rearrange("p (s m) -> p s m", s=2))),
               (f"s_sgw{sl_}", bump(f"s_sgw{sl_}", 16) and (f"s_sgw{sl_}", 16))[1])
            nw[sl_] += 1
            sg_gate[(kind, ms)] = tuple(16 * x for x in nw)
            yield

    slab_sched = [emit_slabs(0)]
    for _ in gen_A(0):
        pass
    for _ in gen_scan(0):
        pass
    for c in range(NCH):
        sWg, sWo, sffn, sh1, sh2, sw3 = slab_sched[c]
        K_x3fm = emit_C(c, sWg, sWo)
        if c + 1 < NCH:
            slab_sched.append(emit_slabs(c + 1))
            a_gen = gen_A(c + 1)
            s_gen = gen_scan(c + 1)
        else:
            a_gen = s_gen = None
        d_gen = gen_D(c, K_x3fm, sffn, sh1, sh2, sw3)
        i = 0
        a_done = s_done = a_gen is None
        for _ in d_gen:
            i += 1
            if not a_done and i % 2 == 0:
                try:
                    next(a_gen)
                except StopIteration:
                    a_done = True
            elif not s_done and a_done and i % 2 == 0:
                try:
                    next(s_gen)
                except StopIteration:
                    s_done = True
        while not a_done:
            try:
                next(a_gen)
            except StopIteration:
                a_done = True
        while not s_done:
            try:
                next(s_gen)
            except StopIteration:
                s_done = True

    W("gpsimd", "s_st", CNT["s_st"])

    final = dict(CNT)
    for e in ENGINES:
        for item in plan[e]:
            if item[0] == "w":
                assert item[2] <= final[item[1]], (e, item)

    with nc.Block() as block:
        def replay(eng_name):
            def f(E):
                for item in plan[eng_name]:
                    if item[0] == "w":
                        E.wait_ge(SEM[item[1]], item[2])
                    else:
                        ins = item[1](E)
                        if item[2] is not None:
                            ins.then_inc(SEM[item[2][0]], item[2][1])
            return f
        block.tensor(replay("tensor"))
        block.scalar(replay("scalar"))
        block.vector(replay("vector"))
        block.sync(replay("sync"))
        block.gpsimd(replay("gpsimd"))

    es.close()
    return nc


def kernel(**inputs):
    nc = build()
    identf = np.eye(128, dtype=np.float32)
    base = {
        "W_in": inputs["L0_attn_W_in"], "b_in": inputs["L0_attn_b_in"],
        "W_gate": inputs["L0_attn_W_gate"], "b_gate": inputs["L0_attn_b_gate"],
        "A": inputs["L0_attn_A"], "Bm": inputs["L0_attn_Bm"],
        "Cw": inputs["L0_attn_C"], "Dw": inputs["L0_attn_D"],
        "W_out": inputs["L0_attn_W_out"], "b_out": inputs["L0_attn_b_out"],
        "w1w": inputs["L0_ffn_w1_w"], "w1s": inputs["L0_ffn_w1_s"],
        "w2w": inputs["L0_ffn_w2_w"], "w2s": inputs["L0_ffn_w2_s"],
        "h1w": inputs["L0_ffn_h1_w"], "h1s": inputs["L0_ffn_h1_s"],
        "h2w": inputs["L0_ffn_h2_w"], "h2s": inputs["L0_ffn_h2_s"],
        "w3w": inputs["L0_ffn_w3_w"], "w3s": inputs["L0_ffn_w3_s"],
        "identf": identf,
    }
    base = {k: np.ascontiguousarray(np.asarray(v, dtype=np.float32)) for k, v in base.items()}
    x_full = np.ascontiguousarray(np.asarray(inputs["L0_attn_x"], dtype=np.float32))
    in_maps = []
    for i in range(N_CORES):
        m = dict(base)
        m["x"] = x_full[2 * i: 2 * i + 2]
        in_maps.append(m)
    res = run_bass_kernel_spmd(nc, in_maps, core_ids=list(range(N_CORES)))
    out = np.empty((16, T, D), dtype=np.float32)
    for i in range(N_CORES):
        out[2 * i: 2 * i + 2] = np.asarray(res.results[i]["y"], dtype=np.float32)
    return out
